# version 10
# baseline (speedup 1.0000x reference)
import dataclasses
import math
from contextlib import ExitStack

import numpy as np

import concourse.bass as bass
import concourse.mybir as mybir
from concourse.bass_utils import run_bass_kernel_spmd

F32 = mybir.dt.float32
F16 = mybir.dt.float16
I32 = mybir.dt.int32
AF = mybir.ActivationFunctionType
ALU = mybir.AluOpType

N_CORES = 8
B, T, C = 16, 2048, 1024
H, KV, HD = 16, 4, 64
N_SEQ = B // N_CORES
N_TB = T // 128
NCH = C // 128
EPS_LN = 1e-5
EPS_QK = 1e-6
SC = 1.0 / math.sqrt(HD)
SHI = np.float32(2 * np.pi / 1024.0)
def _slo():
    d = np.float64(np.pi) / 512 - np.float64(SHI)
    lo = np.float32(d)
    while 512.0 * (np.float64(SHI) + np.float64(lo)) > np.float64(np.pi):
        lo = np.nextafter(lo, np.float32(-np.inf))
    return lo
SLO = _slo()


def ap3(t, off, dims):
    base = t[:]
    return dataclasses.replace(base, offset=base.offset + off,
                               ap=[base.ap[0]] + [list(d) for d in dims])


def apP(t, p0, np_, off, dims):
    base = t[p0:p0 + np_]
    return dataclasses.replace(base, offset=base.offset + off,
                               ap=[base.ap[0]] + [list(d) for d in dims])


class Buf:
    __slots__ = ("name", "writer", "readers")

    def __init__(self, name):
        self.name = name
        self.writer = {}
        self.readers = {}


class Ctx:
    def __init__(self, nc, es):
        self.nc = nc
        self.es = es
        self.eng = {"t": nc.tensor, "v": nc.vector, "s": nc.scalar,
                    "g": nc.gpsimd, "y": nc.sync}
        self.sems = {}
        self.count = {}
        self.waited = {k: {} for k in ("t", "v", "s", "g", "y")}
        qnames = ["qw"] + [f"qx{i}" for i in range(10)] + [f"qo{i}" for i in range(10)]
        for name in ["t", "v", "s", "g"] + qnames:
            self.sems[name] = es.enter_context(nc.semaphore("sem_" + name))
            self.count[name] = 0
        self.qeng = {"qw": "g"}
        for i in range(10):
            self.qeng[f"qx{i}"] = "y"
            self.qeng[f"qo{i}"] = "s"
        self.fencetok = {}

    def fence(self):
        self.fencetok = {s: v for s, v in self.count.items() if v > 0}

    def _waits(self, ename, need):
        w = self.waited[ename]
        for s in sorted(need):
            v = need[s]
            if s == ename and v > self.count[s]:
                continue
            if v > w.get(s, 0):
                self.eng[ename].wait_ge(self.sems[s], v)
                w[s] = v

    def _collect(self, reads, writes):
        need = dict(self.fencetok)
        for b in reads:
            for s, v in b.writer.items():
                if v > need.get(s, 0):
                    need[s] = v
        for b in writes:
            for d in (b.writer, b.readers):
                for s, v in d.items():
                    if v > need.get(s, 0):
                        need[s] = v
        return need

    def op(self, ename, fn, reads=(), writes=(), inc=1):
        self._waits(ename, self._collect(reads, writes))
        inst = fn(self.eng[ename])
        if inc:
            self.count[ename] += 1
            inst.then_inc(self.sems[ename], 1)
        tok = self.count[ename] + (0 if inc else 1)
        for b in writes:
            b.writer = {ename: tok}
            b.readers = {}
        for b in reads:
            if b.readers.get(ename, 0) < tok:
                b.readers[ename] = tok
        return inst

    def dma(self, q, fn, reads=(), writes=()):
        ename = self.qeng[q]
        need = self._collect(reads, writes)
        if self.count[q] > need.get(q, 0):
            need[q] = self.count[q]
        self._waits(ename, need)
        inst = fn(self.eng[ename])
        self.count[q] += 16
        inst.then_inc(self.sems[q], 16)
        tok = self.count[q]
        for b in writes:
            b.writer = {q: tok}
            b.readers = {}
        for b in reads:
            if b.readers.get(q, 0) < tok:
                b.readers[q] = tok

    def sb(self, name, shape, dtype):
        t = self.es.enter_context(self.nc.sbuf_tensor(name, shape, dtype))
        return t, Buf(name)

    def ps(self, name, shape, dtype=F32):
        t = self.es.enter_context(self.nc.psum_tensor(name, shape, dtype))
        return t, Buf(name)


def build_program(n_seq=N_SEQ, n_tb=N_TB):
    TT = n_tb * 128
    nc = bass.Bass("TRN2", target_bir_lowering=False, debug=False)

    d_x = nc.dram_tensor("x", [n_seq, TT, C], F32, kind="ExternalInput")
    d_wqt = nc.dram_tensor("wqt", [C, C], F32, kind="ExternalInput")
    d_wkvt = nc.dram_tensor("wkvt", [C, 512], F32, kind="ExternalInput")
    d_wpt = nc.dram_tensor("wpt", [C, C], F32, kind="ExternalInput")
    d_wr = nc.dram_tensor("wr", [C, C], F32, kind="ExternalInput")
    d_wi = nc.dram_tensor("wi", [C, C], F32, kind="ExternalInput")
    d_bias = nc.dram_tensor("bias", [1, C], F32, kind="ExternalInput")
    d_cos = nc.dram_tensor("cos", [TT, HD], F32, kind="ExternalInput")
    d_sinp = nc.dram_tensor("sinp", [TT, HD], F32, kind="ExternalInput")
    d_out = nc.dram_tensor("out", [n_seq, TT, C], F32, kind="ExternalOutput")

    es = ExitStack()
    cx = Ctx(nc, es)

    wq, b_wq = cx.sb("wq16", [128, NCH * C], F16)
    wkv, b_wkv = cx.sb("wkv16", [128, NCH * 512], F16)
    wp, b_wp = cx.sb("wp16", [128, NCH * C], F16)
    mx, b_mx = cx.sb("mx16", [128, NCH * C], F16)
    cost, b_cos = cx.sb("cos16", [128, n_tb * HD], F16)
    sint, b_sin = cx.sb("sinp16", [128, n_tb * HD], F16)
    kt = [None, None]
    b_ktslots = [[Buf(f"kts{r}_{i}") for i in range(n_tb)] for r in range(2)]
    for reg in range(2):
        kt[reg] = es.enter_context(nc.sbuf_tensor(f"ktil{reg}", [128, TT], F16))
    vpl, _ = cx.sb("vplus", [128, n_tb * 260], F16)
    b_vpslots = [Buf(f"vps{i}") for i in range(n_tb)]
    bias16, b_bias = cx.sb("bias16", [1, C], F16)
    ones1, b_ones1 = cx.sb("ones1", [1, 128], F16)
    ident, b_ident = cx.sb("ident", [128, 128], F16)
    m0, b_m0 = cx.sb("mask0", [128, 128], F16)
    m4, b_m4 = cx.sb("mask4", [128, 128], F16)

    sring, _ = cx.ps("sring", [128, 3 * 512])
    b_slot = [Buf(f"slot{i}") for i in range(3)]
    avps, _ = cx.ps("avps", [128, 1024])
    b_avbank = [Buf("avb0"), Buf("avb1")]
    ptps, b_pt = cx.ps("ptps", [128, 1024], F16)
    prj, _ = cx.ps("prj", [128, 1024])
    b_prjb = [Buf("prj0"), Buf("prj1")]

    def t_(fn, reads=(), writes=(), inc=1):
        return cx.op("t", fn, reads, writes, inc)

    def v_(fn, reads=(), writes=(), inc=1):
        return cx.op("v", fn, reads, writes, inc)

    def s_(fn, reads=(), writes=(), inc=1):
        return cx.op("s", fn, reads, writes, inc)

    def g_(fn, reads=(), writes=(), inc=1):
        return cx.op("g", fn, reads, writes, inc)

    pro_es = ExitStack()
    kio, b_kio = (pro_es.enter_context(nc.sbuf_tensor("kio", [128, C], F32)), Buf("kio"))
    jcol, b_jcol = (pro_es.enter_context(nc.sbuf_tensor("jcol", [128, 1], F32)), Buf("jcol"))
    g_(lambda e: e.iota(kio[:], pattern=[[1, C]], base=0, channel_multiplier=0,
                        allow_small_or_imprecise_dtypes=True), writes=[b_kio])
    g_(lambda e: e.iota(jcol[:], pattern=[[0, 1]], base=0, channel_multiplier=1,
                        allow_small_or_imprecise_dtypes=True), writes=[b_jcol])
    v_(lambda e: e.tensor_scalar(out=ident[:], in0=kio[:, 0:128], scalar1=jcol[:],
                                 scalar2=None, op0=ALU.is_equal),
       reads=[b_kio, b_jcol], writes=[b_ident])
    v_(lambda e: e.tensor_scalar(out=m0[:], in0=kio[:, 0:128], scalar1=jcol[:],
                                 scalar2=None, op0=ALU.is_ge),
       reads=[b_kio, b_jcol], writes=[b_m0])
    v_(lambda e: e.tensor_scalar(out=m4[:], in0=kio[:, 0:128], scalar1=jcol[:],
                                 scalar2=None, op0=ALU.is_lt),
       reads=[b_kio, b_jcol], writes=[b_m4])
    v_(lambda e: e.memset(ones1[:], 1.0), writes=[b_ones1])

    cx.dma("qw", lambda e: e.dma_start(
        out=ap3(wq, 0, [[C, NCH], [1, C]]),
        in_=d_wqt.ap().rearrange("(cc p) i -> p cc i", p=128)), writes=[b_wq])
    cx.dma("qw", lambda e: e.dma_start(
        out=ap3(wkv, 0, [[512, NCH], [1, 512]]),
        in_=d_wkvt.ap().rearrange("(cc p) j -> p cc j", p=128)), writes=[b_wkv])
    cx.dma("qw", lambda e: e.dma_start(
        out=ap3(wp, 0, [[C, NCH], [1, C]]),
        in_=d_wpt.ap().rearrange("(ic p) o -> p ic o", p=128)), writes=[b_wp])
    cx.dma("qw", lambda e: e.dma_start(
        out=ap3(cost, 0, [[HD, n_tb], [1, HD]]),
        in_=d_cos.ap().rearrange("(tb p) d -> p tb d", p=128)), writes=[b_cos])
    cx.dma("qw", lambda e: e.dma_start(
        out=ap3(sint, 0, [[HD, n_tb], [1, HD]]),
        in_=d_sinp.ap().rearrange("(tb p) d -> p tb d", p=128)), writes=[b_sin])
    cx.dma("qw", lambda e: e.dma_start(out=bias16[:], in_=d_bias.ap()),
           writes=[b_bias])

    with ExitStack() as mes:
        def msb(nm, sh, dt):
            return mes.enter_context(nc.sbuf_tensor(nm, sh, dt)), Buf(nm)

        c16, b_c16 = msb("c16", [128, NCH * C], F16)
        s16, b_s16 = msb("s16", [128, NCH * C], F16)
        wr16, b_wr16 = msb("wr16", [128, NCH * C], F16)
        wi16, b_wi16 = msb("wi16", [128, NCH * C], F16)
        pt16, b_pt16 = msb("ptm16", [128, NCH * C], F16)
        qt16, b_qt16 = msb("qtm16", [128, NCH * C], F16)
        jk, b_jk = msb("jk", [128, C], F32)
        red, b_red = msb("red", [128, C], F32)
        scr, b_scr = msb("mscr", [128, C], F32)
        qi, b_qi = msb("qi", [128, C], I32)
        ang, b_ang = msb("angm", [128, C], F32)

        cx.dma("qw", lambda e: e.dma_start(
            out=ap3(wr16, 0, [[C, NCH], [1, C]]),
            in_=d_wr.ap().rearrange("(kc p) o -> p kc o", p=128)), writes=[b_wr16])
        cx.dma("qw", lambda e: e.dma_start(
            out=ap3(wi16, 0, [[C, NCH], [1, C]]),
            in_=d_wi.ap().rearrange("(kc p) o -> p kc o", p=128)), writes=[b_wi16])

        for cc in range(NCH):
            v_(lambda e, cc=cc: e.tensor_scalar(
                out=scr[:, 0:1], in0=jcol[:], scalar1=float(128 * cc),
                scalar2=None, op0=ALU.add), reads=[b_jcol], writes=[b_scr])
            v_(lambda e: e.tensor_scalar(
                out=jk[:], in0=kio[:], scalar1=scr[:, 0:1], scalar2=None,
                op0=ALU.mult), reads=[b_kio, b_scr], writes=[b_jk])
            v_(lambda e: e.tensor_scalar(
                out=qi[:], in0=jk[:], scalar1=float(2.0 ** -10),
                scalar2=float(0.5 + 2.0 ** -12), op0=ALU.mult, op1=ALU.add),
               reads=[b_jk], writes=[b_qi])
            v_(lambda e: e.tensor_copy(out=red[:], in_=qi[:]),
               reads=[b_qi], writes=[b_red])
            v_(lambda e: e.scalar_tensor_tensor(
                out=red[:], in0=red[:], scalar=-1024.0, in1=jk[:],
                op0=ALU.mult, op1=ALU.add), reads=[b_red, b_jk], writes=[b_red])
            v_(lambda e: e.tensor_single_scalar(
                out=scr[:], in_=red[:], scalar=512.0, op=ALU.is_ge),
               reads=[b_red], writes=[b_scr])
            v_(lambda e: e.scalar_tensor_tensor(
                out=red[:], in0=scr[:], scalar=-1024.0, in1=red[:],
                op0=ALU.mult, op1=ALU.add), reads=[b_scr, b_red], writes=[b_red])
            v_(lambda e: e.tensor_single_scalar(
                out=scr[:], in_=red[:], scalar=-512.0, op=ALU.is_lt),
               reads=[b_red], writes=[b_scr])
            v_(lambda e: e.scalar_tensor_tensor(
                out=red[:], in0=scr[:], scalar=1024.0, in1=red[:],
                op0=ALU.mult, op1=ALU.add), reads=[b_scr, b_red], writes=[b_red])
            v_(lambda e: e.tensor_scalar(
                out=ang[:], in0=red[:], scalar1=float(SHI), scalar2=None,
                op0=ALU.mult), reads=[b_red], writes=[b_ang])
            v_(lambda e: e.scalar_tensor_tensor(
                out=ang[:], in0=red[:], scalar=float(SLO), in1=ang[:],
                op0=ALU.mult, op1=ALU.add), reads=[b_red, b_ang], writes=[b_ang])
            s_(lambda e, cc=cc: e.activation(
                out=s16[:, cc * C:(cc + 1) * C], in_=ang[:], func=AF.Sin),
               reads=[b_ang], writes=[b_s16])
            v_(lambda e: e.tensor_single_scalar(
                out=scr[:], in_=red[:], scalar=256.0, op=ALU.is_ge),
               reads=[b_red], writes=[b_scr])
            v_(lambda e: e.scalar_tensor_tensor(
                out=red[:], in0=scr[:], scalar=-1024.0, in1=red[:],
                op0=ALU.mult, op1=ALU.add), reads=[b_scr, b_red], writes=[b_red])
            v_(lambda e: e.tensor_scalar(
                out=red[:], in0=red[:], scalar1=256.0, scalar2=None,
                op0=ALU.add), reads=[b_red], writes=[b_red])
            v_(lambda e: e.tensor_scalar(
                out=ang[:], in0=red[:], scalar1=float(SHI), scalar2=None,
                op0=ALU.mult), reads=[b_red], writes=[b_ang])
            v_(lambda e: e.scalar_tensor_tensor(
                out=ang[:], in0=red[:], scalar=float(SLO), in1=ang[:],
                op0=ALU.mult, op1=ALU.add), reads=[b_red, b_ang], writes=[b_ang])
            s_(lambda e, cc=cc: e.activation(
                out=c16[:, cc * C:(cc + 1) * C], in_=ang[:], func=AF.Sin),
               reads=[b_ang], writes=[b_c16])
        def negate_wi():
            v_(lambda e: e.tensor_scalar(out=wi16[:], in0=wi16[:], scalar1=-1.0,
                                         scalar2=None, op0=ALU.mult),
               reads=[b_wi16], writes=[b_wi16])
        for dst16, b_dst, terms in (
            (pt16, b_pt16, [(wr16, b_wr16, c16, b_c16), (wi16, b_wi16, s16, b_s16)]),
            (qt16, b_qt16, [(wi16, b_wi16, c16, b_c16), (wr16, b_wr16, s16, b_s16)]),
        ):
            if dst16 is qt16:
                negate_wi()
            for oc in range(NCH):
                for ns in range(2):
                    k_i = 0
                    nmm = 2 * NCH
                    for (wsb, b_w, msb_, b_m) in terms:
                        for kc in range(NCH):
                            last = k_i == nmm - 1
                            t_(lambda e, oc=oc, ns=ns, kc=kc, wsb=wsb, msb_=msb_,
                               first=(k_i == 0), last=last: e.matmul(
                                   prj[:, ns * 512:(ns + 1) * 512],
                                   wsb[:, kc * C + oc * 128: kc * C + oc * 128 + 128],
                                   msb_[:, kc * C + ns * 512: kc * C + (ns + 1) * 512],
                                   start=first, stop=last),
                               reads=[b_w, b_m], writes=[b_prjb[ns]],
                               inc=1 if last else 0)
                            k_i += 1
                    s_(lambda e, oc=oc, ns=ns, dst16=dst16: e.activation(
                        out=dst16[:, oc * C + ns * 512: oc * C + (ns + 1) * 512],
                        in_=prj[:, ns * 512:(ns + 1) * 512], func=AF.Copy,
                        scale=float(2.0 ** -10)),
                       reads=[b_prjb[ns]], writes=[b_dst])
        for jc in range(NCH):
            for ns in range(2):
                k_i = 0
                for (lsb, b_l, msb_, b_m) in ((pt16, b_pt16, c16, b_c16),
                                              (qt16, b_qt16, s16, b_s16)):
                    for oc in range(NCH):
                        last = k_i == 2 * NCH - 1
                        t_(lambda e, jc=jc, ns=ns, oc=oc, lsb=lsb, msb_=msb_,
                           first=(k_i == 0), last=last: e.matmul(
                               prj[:, ns * 512:(ns + 1) * 512],
                               lsb[:, oc * C + jc * 128: oc * C + jc * 128 + 128],
                               msb_[:, oc * C + ns * 512: oc * C + (ns + 1) * 512],
                               start=first, stop=last),
                           reads=[b_l, b_m], writes=[b_prjb[ns]],
                           inc=1 if last else 0)
                        k_i += 1
                s_(lambda e, jc=jc, ns=ns: e.activation(
                    out=mx[:, jc * C + ns * 512: jc * C + (ns + 1) * 512],
                    in_=prj[:, ns * 512:(ns + 1) * 512], func=AF.Copy),
                   reads=[b_prjb[ns]], writes=[b_mx])
    pro_es.close()
    cx.fence()

    NXB = 9
    xb, b_xb = zip(*[cx.sb(f"xb{i}", [128, C], F32) for i in range(NXB)])
    lnT, b_lnT = zip(*[cx.sb(f"lnT{i}", [128, C], F16) for i in range(2)])
    ln2T, b_ln2T = zip(*[cx.sb(f"ln2T{i}", [128, C], F16) for i in range(2)])
    lntm, b_lntm = zip(*[cx.sb(f"lntm{i}", [128, C], F16) for i in range(4)])
    ln2tm, b_ln2tm = zip(*[cx.sb(f"ln2tm{i}", [128, C], F16) for i in range(3)])
    qtm, b_qtm = zip(*[cx.sb(f"qtm{i}", [128, C], F16) for i in range(3)])
    qtm2, b_qtm2 = zip(*[cx.sb(f"qtm2_{i}", [128, C], F16) for i in range(3)])
    qfm, b_qfm = zip(*[cx.sb(f"qfm{i}", [128, C], F16) for i in range(3)])
    ktm, b_ktm = zip(*[cx.sb(f"ktm{i}", [128, 256], F16) for i in range(3)])
    ktm2, b_ktm2 = zip(*[cx.sb(f"ktm2_{i}", [128, 256], F16) for i in range(3)])
    sscr, b_sscr = zip(*[cx.sb(f"sscr{i}", [128, C], F32) for i in range(2)])
    sdum, b_sdum = cx.sb("sdum", [128, C], F32)
    exs, b_exs = zip(*[cx.sb(f"exs{i}", [128, 2 * 2560], F16) for i in range(2)])
    b_exsg = [[Buf(f"exsg{i}_{g}") for g in range(2)] for i in range(2)]
    av16, b_av16 = zip(*[cx.sb(f"av16_{i}", [128, C], F16) for i in range(3)])
    avfm, b_avfm = zip(*[cx.sb(f"avfm{i}", [128, C], F16) for i in range(2)])
    st_, b_st = zip(*[cx.sb(f"stats{i}", [128, 16], F32) for i in range(8)])
    qss, b_qss = zip(*[cx.sb(f"qss{i}", [128, 24], F32) for i in range(2)])
    rcq, b_rcq = zip(*[cx.sb(f"rcq{i}", [128, 24], F32) for i in range(2)])
    rdn, b_rdn = zip(*[cx.sb(f"rdn{i}", [128, 16], F32) for i in range(2)])


    n_ticks = n_seq * n_tb

    def x_load(tau):
        s, i = divmod(tau, n_tb)
        xi = tau % NXB
        cx.dma(f"qx{xi}", lambda e: e.dma_start(
            out=xb[xi][:], in_=d_x.ap()[s, i * 128:(i + 1) * 128, :]),
            writes=[b_xb[xi]])


    def ln_stats(xbuf, b_xbuf, scrt, b_scrt, sti):
        stt, b_stt = st_[sti], b_st[sti]
        s_(lambda e: e.activation(out=scrt[:], in_=xbuf[:], func=AF.Square,
                                  accum_out=stt[:, 0:1]),
           reads=[b_xbuf], writes=[b_scrt, b_stt])
        s_(lambda e: e.activation(out=scrt[:], in_=xbuf[:], func=AF.Copy,
                                  accum_out=stt[:, 1:2]),
           reads=[b_xbuf], writes=[b_scrt, b_stt])
        s_(lambda e: e.activation(out=stt[:, 2:3], in_=stt[:, 1:2], func=AF.Copy,
                                  scale=float(2.0 ** -10)),
           reads=[b_stt], writes=[b_stt])
        s_(lambda e: e.activation(out=stt[:, 3:4], in_=stt[:, 2:3],
                                  func=AF.Square),
           reads=[b_stt], writes=[b_stt])
        s_(lambda e: e.activation(out=stt[:, 4:5], in_=stt[:, 3:4], func=AF.Copy,
                                  scale=-1.0, bias=float(EPS_LN)),
           reads=[b_stt], writes=[b_stt])
        s_(lambda e: e.activation(out=stt[:, 6:7], in_=stt[:, 0:1], func=AF.Ln,
                                  scale=float(2.0 ** -10), bias=stt[:, 4:5]),
           reads=[b_stt], writes=[b_stt])
        s_(lambda e: e.activation(out=stt[:, 7:8], in_=stt[:, 6:7], func=AF.Exp,
                                  scale=-0.5), reads=[b_stt], writes=[b_stt])
        s_(lambda e: e.activation(out=stt[:, 9:10], in_=stt[:, 2:3], func=AF.Copy,
                                  scale=stt[:, 7:8]),
           reads=[b_stt], writes=[b_stt])

    def ln_apply(xbuf, b_xbuf, dst, b_dst, sti):
        stt, b_stt = st_[sti], b_st[sti]
        v_(lambda e: e.tensor_scalar(out=dst[:], in0=xbuf[:], scalar1=stt[:, 7:8],
                                     scalar2=stt[:, 9:10], op0=ALU.mult,
                                     op1=ALU.subtract),
           reads=[b_xbuf, b_stt], writes=[b_dst])

    def transpose8(srct, b_src, dstt, b_dst, nch=NCH):
        for ccc in range(nch):
            t_(lambda e, ccc=ccc: e.transpose(
                ptps[:, ccc * 128:(ccc + 1) * 128],
                srct[:, ccc * 128:(ccc + 1) * 128], ident[:]),
               reads=[b_src, b_ident], writes=[b_pt],
               inc=1 if ccc == nch - 1 else 0)
        v_(lambda e: e.tensor_copy(out=dstt[:, 0:nch * 128],
                                   in_=ptps[:, 0:nch * 128]),
           reads=[b_pt], writes=[b_dst])

    def stage_ln2(ptau):
        pr = ptau % 2
        q3 = ptau % 3
        pxi = ptau % NXB
        ln_stats(xb[pxi], b_xb[pxi], sdum, b_sdum, 4 + pr)
        ln_apply(xb[pxi], b_xb[pxi], ln2tm[q3], b_ln2tm[q3], 4 + pr)

    def stage_ffn_out(ptau):
        pr = ptau % 2
        q3 = ptau % 3
        ps_, pi = divmod(ptau, n_tb)
        pxi = ptau % NXB
        transpose8(ln2tm[q3], b_ln2tm[q3], ln2T[pr], b_ln2T[pr])
        for ns in range(2):
            for ccc in range(NCH):
                t_(lambda e, ccc=ccc, ns=ns, pr=pr: e.matmul(
                    prj[:, ns * 512:(ns + 1) * 512],
                    ln2T[pr][:, ccc * 128:(ccc + 1) * 128],
                    mx[:, ccc * C + ns * 512: ccc * C + (ns + 1) * 512],
                    start=(ccc == 0), stop=False),
                   reads=[b_ln2T[pr], b_mx], writes=[b_prjb[ns]], inc=0)
            t_(lambda e, ns=ns: e.matmul(
                prj[:, ns * 512:(ns + 1) * 512], ones1[:],
                bias16[:, ns * 512:(ns + 1) * 512], start=False, stop=True),
               reads=[b_ones1, b_bias], writes=[b_prjb[ns]], inc=1)
            v_(lambda e, ns=ns, pxi=pxi: e.tensor_tensor(
                out=xb[pxi][:, ns * 512:(ns + 1) * 512],
                in0=prj[:, ns * 512:(ns + 1) * 512],
                in1=xb[pxi][:, ns * 512:(ns + 1) * 512], op=ALU.add),
               reads=[b_prjb[ns], b_xb[pxi]], writes=[b_xb[pxi]])
        cx.dma(f"qo{pxi}", lambda e, ps_=ps_, pi=pi, pxi=pxi: e.dma_start(
            out=d_out.ap()[ps_, pi * 128:(pi + 1) * 128, :], in_=xb[pxi][:]),
            reads=[b_xb[pxi]])

    slot_ctr = [0]

    def stage_ln1t(tau):
        r = tau % 2
        transpose8(lntm[tau % 4], b_lntm[tau % 4], lnT[r], b_lnT[r])

    def stage_qkv(tau):
        s, i = divmod(tau, n_tb)
        r = tau % 2
        q3 = tau % 3
        xi = tau % NXB
        for ns in range(2):
            for ccc in range(NCH):
                t_(lambda e, ns=ns, ccc=ccc, r=r: e.matmul(
                    prj[:, ns * 512:(ns + 1) * 512],
                    lnT[r][:, ccc * 128:(ccc + 1) * 128],
                    wq[:, ccc * C + ns * 512: ccc * C + (ns + 1) * 512],
                    start=(ccc == 0), stop=(ccc == NCH - 1)),
                   reads=[b_lnT[r], b_wq], writes=[b_prjb[ns]],
                   inc=1 if ccc == NCH - 1 else 0)
            v_(lambda e, ns=ns, r=r: e.tensor_copy(
                out=qtm[q3][:, ns * 512:(ns + 1) * 512],
                in_=prj[:, ns * 512:(ns + 1) * 512]),
               reads=[b_prjb[ns]], writes=[b_qtm[q3]])
        for ccc in range(NCH):
            t_(lambda e, ccc=ccc, r=r: e.matmul(
                prj[:, 0:512], lnT[r][:, ccc * 128:(ccc + 1) * 128],
                wkv[:, ccc * 512:(ccc + 1) * 512],
                start=(ccc == 0), stop=(ccc == NCH - 1)),
               reads=[b_lnT[r], b_wkv], writes=[b_prjb[0]],
               inc=1 if ccc == NCH - 1 else 0)
        v_(lambda e, r=r: e.tensor_copy(out=ktm[q3][:, 0:256], in_=prj[:, 0:256]),
           reads=[b_prjb[0]], writes=[b_ktm[q3]])
        v_(lambda e, i=i: e.tensor_copy(
            out=ap3(vpl, i * 260, [[65, 4], [1, 64]]),
            in_=dataclasses.replace(prj[:], offset=prj[:].offset + 256,
                                    ap=[prj[:].ap[0], [64, 4], [1, 64]])),
           reads=[b_prjb[0]], writes=[b_vpslots[i]])
        v_(lambda e, i=i: e.memset(ap3(vpl, i * 260 + 64, [[65, 4], [1, 1]]), 1.0),
           writes=[b_vpslots[i]])


    def stage_norm_rope(tau):
        s, i = divmod(tau, n_tb)
        r = tau % 2
        q3 = tau % 3
        qs, b_qs = qss[r], b_qss[r]
        rc, b_rc = rcq[r], b_rcq[r]
        sc2, b_sc2 = sscr[r], b_sscr[r]
        s_(lambda e, r=r: e.activation(out=sc2[:], in_=qtm[q3][:], func=AF.Square),
           reads=[b_qtm[q3]], writes=[b_sc2])
        v_(lambda e, r=r: e.tensor_reduce(
            out=ap3(qss[r], 0, [[1, 16], [1, 1]]),
            in_=ap3(sscr[r], 0, [[64, 16], [1, 64]]),
            axis=mybir.AxisListType.X, op=ALU.add),
           reads=[b_sc2], writes=[b_qs])
        s_(lambda e, r=r: e.activation(out=sc2[:, 0:256], in_=ktm[q3][:, 0:256],
                                       func=AF.Square),
           reads=[b_ktm[q3]], writes=[b_sc2])
        v_(lambda e, r=r: e.tensor_reduce(
            out=ap3(qss[r], 16, [[1, 4], [1, 1]]),
            in_=ap3(sscr[r], 0, [[64, 4], [1, 64]]),
            axis=mybir.AxisListType.X, op=ALU.add),
           reads=[b_sc2], writes=[b_qs])
        s_(lambda e, r=r: e.activation(out=ap3(qss[r], 0, [[1, 20]]),
                                       in_=ap3(qss[r], 0, [[1, 20]]),
                                       func=AF.Sqrt),
           reads=[b_qs], writes=[b_qs])
        v_(lambda e, r=r: e.tensor_scalar(out=ap3(qss[r], 0, [[1, 20]]),
                                          in0=ap3(qss[r], 0, [[1, 20]]),
                                          scalar1=float(EPS_QK), scalar2=None,
                                          op0=ALU.add), reads=[b_qs], writes=[b_qs])
        v_(lambda e, r=r: e.reciprocal(out=ap3(rcq[r], 0, [[1, 20]]),
                                       in_=ap3(qss[r], 0, [[1, 20]])),
           reads=[b_qs], writes=[b_rc])
        v_(lambda e, r=r: e.tensor_scalar(out=ap3(rcq[r], 0, [[1, 16]]),
                                          in0=ap3(rcq[r], 0, [[1, 16]]),
                                          scalar1=float(SC), scalar2=None,
                                          op0=ALU.mult), reads=[b_rc], writes=[b_rc])
        for (srcb, b_srcb, dstb, b_dstb, nh, roff) in (
                (qtm[q3], b_qtm[q3], qtm2[q3], b_qtm2[q3], 16, 0),
                (ktm[q3], b_ktm[q3], ktm2[q3], b_ktm2[q3], 4, 16)):
            v_(lambda e, srcb=srcb, dstb=dstb, nh=nh, roff=roff: e.tensor_tensor(
                out=ap3(dstb, 0, [[64, nh], [1, 64]]),
                in0=ap3(srcb, 0, [[64, nh], [1, 64]]),
                in1=ap3(rc, roff, [[1, nh], [0, 64]]), op=ALU.mult),
               reads=[b_srcb, b_rc], writes=[b_dstb])
            v_(lambda e, srcb=srcb, dstb=dstb, nh=nh, i=i: e.tensor_tensor(
                out=ap3(srcb, 0, [[64, nh], [32, 2], [1, 32]]),
                in0=dataclasses.replace(
                    dstb[:], offset=dstb[:].offset + 32,
                    ap=[dstb[:].ap[0], [64, nh], [-32, 2], [1, 32]]),
                in1=ap3(sint, i * HD, [[0, nh], [32, 2], [1, 32]]),
                op=ALU.mult),
               reads=[b_dstb, b_sin], writes=[b_srcb])
            v_(lambda e, dstb=dstb, nh=nh, i=i: e.tensor_tensor(
                out=ap3(dstb, 0, [[64, nh], [1, 64]]),
                in0=ap3(dstb, 0, [[64, nh], [1, 64]]),
                in1=ap3(cost, i * HD, [[0, nh], [1, 64]]), op=ALU.mult),
               reads=[b_dstb, b_cos], writes=[b_dstb])
            v_(lambda e, srcb=srcb, dstb=dstb, nh=nh: e.tensor_tensor(
                out=ap3(dstb, 0, [[64, nh], [1, 64]]),
                in0=ap3(dstb, 0, [[64, nh], [1, 64]]),
                in1=ap3(srcb, 0, [[64, nh], [1, 64]]), op=ALU.add),
               reads=[b_dstb, b_srcb], writes=[b_dstb])


    def stage_pre_b(tau):
        s, i = divmod(tau, n_tb)
        r = tau % 2
        q3 = tau % 3
        for h in range(H):
            rowb = ((h // 4) % 2) * 64
            strip = (h % 4) + 4 * (h // 8)
            t_(lambda e, h=h, rowb=rowb, strip=strip, r=r: e.transpose(
                dataclasses.replace(ptps[rowb:rowb + 64],
                                    offset=ptps[rowb:rowb + 64].offset + strip * 128,
                                    ap=[ptps[rowb:rowb + 64].ap[0], [1, 128]]),
                qtm2[q3][:, h * 64:(h + 1) * 64], ident[:]),
               reads=[b_qtm2[q3], b_ident], writes=[b_pt],
               inc=1 if h == H - 1 else 0)
        v_(lambda e, r=r: e.tensor_copy(out=qfm[q3][:, 0:C], in_=ptps[:, 0:C]),
           reads=[b_pt], writes=[b_qfm[q3]])
        for reg in range(2):
            t_(lambda e, reg=reg, r=r: e.transpose(
                ptps[:, reg * 128:(reg + 1) * 128],
                ktm2[q3][:, reg * 128:(reg + 1) * 128], ident[:]),
               reads=[b_ktm2[q3], b_ident], writes=[b_pt],
               inc=1 if reg == 1 else 0)
        for reg in range(2):
            v_(lambda e, reg=reg, i=i: e.tensor_copy(
                out=kt[reg][:, i * 128:(i + 1) * 128],
                in_=ptps[:, reg * 128:(reg + 1) * 128]),
               reads=[b_pt], writes=[b_ktslots[reg][i]])

    def stage_att(tau):
        s, i = divmod(tau, n_tb)
        r = tau % 2
        q3 = tau % 3
        xi = tau % NXB
        kcs = list(range(max(0, i - 4), i + 1))
        nkc = len(kcs)
        exsb, b_exsb = exs[r], b_exs[r]

        def emit_scores(g):
            reg, row = g // 2, (g % 2) * 64
            pend = []
            for kidx, kc in enumerate(kcs):
                sl = slot_ctr[0] % 3
                slot_ctr[0] += 1
                for j in range(4):
                    h = 4 * g + j
                    strip = (h % 4) + 4 * (h // 8)
                    t_(lambda e, sl=sl, j=j, reg=reg, row=row, kc=kc,
                       strip=strip, r=r, g=g: e.matmul(
                           sring[:, sl * 512 + j * 128: sl * 512 + (j + 1) * 128],
                           apP(kt[reg], row, 64, kc * 128, [[1, 128]]),
                           apP(qfm[q3], row, 64, strip * 128, [[1, 128]]),
                           start=(j == 0), stop=(j == 3)),
                       reads=[b_ktslots[reg][kc], b_qfm[q3]], writes=[b_slot[sl]],
                       inc=1 if j == 3 else 0)
                pend.append((kidx, sl))
                while len(pend) >= 2:
                    if pend[1][1] == pend[0][1] + 1:
                        k0, s0 = pend[0]
                        del pend[0:2]
                        s_(lambda e, k0=k0, s0=s0, g=g: e.activation(
                            out=exsb[:, (g % 2) * 2560 + k0 * 512:
                                     (g % 2) * 2560 + (k0 + 2) * 512],
                            in_=sring[:, s0 * 512:(s0 + 2) * 512], func=AF.Exp),
                           reads=[b_slot[s0], b_slot[s0 + 1]],
                           writes=[b_exsg[r][g % 2]])
                    else:
                        k0, s0 = pend.pop(0)
                        s_(lambda e, k0=k0, s0=s0, g=g: e.activation(
                            out=exsb[:, (g % 2) * 2560 + k0 * 512:
                                     (g % 2) * 2560 + (k0 + 1) * 512],
                            in_=sring[:, s0 * 512:(s0 + 1) * 512], func=AF.Exp),
                           reads=[b_slot[s0]], writes=[b_exsg[r][g % 2]])
            for k0, s0 in pend:
                s_(lambda e, k0=k0, s0=s0, g=g: e.activation(
                    out=exsb[:, (g % 2) * 2560 + k0 * 512:
                             (g % 2) * 2560 + (k0 + 1) * 512],
                    in_=sring[:, s0 * 512:(s0 + 1) * 512], func=AF.Exp),
                   reads=[b_slot[s0]], writes=[b_exsg[r][g % 2]])
            pend.clear()
            v_(lambda e, kidx=nkc - 1, g=g: e.tensor_tensor(
                out=ap3(exsb, (g % 2) * 2560 + kidx * 512, [[128, 4], [1, 128]]),
                in0=ap3(exsb, (g % 2) * 2560 + kidx * 512, [[128, 4], [1, 128]]),
                in1=ap3(m0, 0, [[0, 4], [1, 128]]), op=ALU.mult),
               reads=[b_exsg[r][g % 2], b_m0], writes=[b_exsg[r][g % 2]])
            if i >= 4:
                v_(lambda e, g=g: e.tensor_tensor(
                    out=ap3(exsb, (g % 2) * 2560, [[128, 4], [1, 128]]),
                    in0=ap3(exsb, (g % 2) * 2560, [[128, 4], [1, 128]]),
                    in1=ap3(m4, 0, [[0, 4], [1, 128]]), op=ALU.mult),
                   reads=[b_exsg[r][g % 2], b_m4], writes=[b_exsg[r][g % 2]])

        def emit_av(g):
            bank = g % 2
            for kidx, kc in enumerate(kcs):
                for j in range(4):
                    first = kidx == 0 and j == 0
                    last = kidx == nkc - 1 and j == 3
                    t_(lambda e, kidx=kidx, kc=kc, j=j, g=g, bank=bank,
                       first=first, last=last: e.matmul(
                           avps[:, bank * 512 + j * 128: bank * 512 + j * 128 + 65],
                           exsb[:, (g % 2) * 2560 + kidx * 512 + j * 128:
                                (g % 2) * 2560 + kidx * 512 + (j + 1) * 128],
                           ap3(vpl, kc * 260 + g * 65, [[1, 65]]),
                           start=first, stop=last),
                       reads=[b_exsg[r][g % 2], b_vpslots[kc]], writes=[b_avbank[bank]],
                       inc=1 if last else 0)
            if g % 2 == 1:
                hp = g // 2
                rd, b_rd = rdn[r], b_rdn[r]
                v_(lambda e, hp=hp, rd=rd: e.reciprocal(
                    out=ap3(rd, hp * 8, [[1, 8]]),
                    in_=dataclasses.replace(
                        avps[:], offset=avps[:].offset + 64,
                        ap=[avps[:].ap[0], [128, 8], [1, 1]])),
                   reads=[b_avbank[0], b_avbank[1]], writes=[b_rd])
                v_(lambda e, hp=hp, rd=rd, r=r: e.tensor_tensor(
                    out=ap3(av16[q3], hp * 512, [[64, 8], [1, 64]]),
                    in0=dataclasses.replace(
                        avps[:], offset=avps[:].offset,
                        ap=[avps[:].ap[0], [128, 8], [1, 64]]),
                    in1=ap3(rd, hp * 8, [[1, 8], [0, 64]]), op=ALU.mult),
                   reads=[b_avbank[0], b_avbank[1], b_rd], writes=[b_av16[q3]])

        emit_scores(0)
        emit_scores(1)
        emit_av(0)
        emit_scores(2)
        emit_av(1)
        emit_scores(3)
        emit_av(2)
        emit_av(3)

    def stage_att_tail(tau):
        r = tau % 2
        q3 = tau % 3
        xi = tau % NXB
        transpose8(av16[q3], b_av16[q3], avfm[r], b_avfm[r])
        for ns in range(2):
            for ccc in range(NCH):
                t_(lambda e, ns=ns, ccc=ccc, r=r: e.matmul(
                    prj[:, ns * 512:(ns + 1) * 512],
                    avfm[r][:, ccc * 128:(ccc + 1) * 128],
                    wp[:, ccc * C + ns * 512: ccc * C + (ns + 1) * 512],
                    start=(ccc == 0), stop=(ccc == NCH - 1)),
                   reads=[b_avfm[r], b_wp], writes=[b_prjb[ns]],
                   inc=1 if ccc == NCH - 1 else 0)
            v_(lambda e, ns=ns, xi=xi: e.tensor_tensor(
                out=xb[xi][:, ns * 512:(ns + 1) * 512],
                in0=prj[:, ns * 512:(ns + 1) * 512],
                in1=xb[xi][:, ns * 512:(ns + 1) * 512], op=ALU.add),
               reads=[b_prjb[ns], b_xb[xi]], writes=[b_xb[xi]])

    def ln1_full(t2):
        r2 = t2 % 2
        ln_stats(xb[t2 % NXB], b_xb[t2 % NXB], sdum, b_sdum, r2)
        ln_apply(xb[t2 % NXB], b_xb[t2 % NXB], lntm[t2 % 4], b_lntm[t2 % 4], r2)

    N = n_ticks

    def guard(t):
        return 0 <= t < N

    for it in range(N + 8):
        if guard(it - 1):
            stage_qkv(it - 1)
        if guard(it - 7):
            stage_ffn_out(it - 7)
        if guard(it):
            x_load(it)
        if guard(it - 4):
            stage_att(it - 4)
        if guard(it - 3):
            stage_pre_b(it - 3)
        if guard(it - 5):
            stage_att_tail(it - 5)
        if guard(it - 6):
            stage_ln2(it - 6)
        if guard(it):
            ln1_full(it)
        if guard(it - 2):
            stage_norm_rope(it - 2)
        if guard(it):
            stage_ln1t(it)

    for i in range(10):
        if cx.count[f"qo{i}"]:
            cx.eng["s"].wait_ge(cx.sems[f"qo{i}"], cx.count[f"qo{i}"])
    es.close()
    return nc


_ROPE_CACHE = {}


def _rope_tables(TT):
    if TT in _ROPE_CACHE:
        return _ROPE_CACHE[TT]
    import jax
    import jax.numpy as jnp
    cpu = jax.devices("cpu")[0]
    with jax.default_device(cpu):
        theta = 1.0 / 10000.0 ** (jnp.arange(0, HD, 2, dtype=jnp.float32) / HD)
        pos = jnp.arange(TT, dtype=jnp.float32)
        freqs = jnp.outer(pos, theta)
        cos = jnp.concatenate([jnp.cos(freqs), jnp.cos(freqs)], axis=-1)
        sin = jnp.concatenate([jnp.sin(freqs), jnp.sin(freqs)], axis=-1)
        cos = np.asarray(jax.device_get(cos)).astype(np.float32)
        sin = np.asarray(jax.device_get(sin)).astype(np.float32)
    sinp = sin.copy()
    sinp[:, :32] = -sinp[:, :32]
    _ROPE_CACHE[TT] = (cos, sinp)
    return cos, sinp


def make_in_maps(L0_attn_x, L0_attn_Wq, L0_attn_Wkv, L0_attn_Wproj,
                 L0_ffn_w_freq_real, L0_ffn_w_freq_imag, L0_ffn_bias,
                 n_cores=N_CORES, n_seq=N_SEQ, n_tb=N_TB):
    TT = n_tb * 128
    cos, sinp = _rope_tables(TT)
    f = np.float32
    wqt = np.ascontiguousarray(np.asarray(L0_attn_Wq, dtype=f).T)
    wkvt = np.ascontiguousarray(np.asarray(L0_attn_Wkv, dtype=f).T)
    wpt = np.ascontiguousarray(np.asarray(L0_attn_Wproj, dtype=f).T)
    wr = np.ascontiguousarray(np.asarray(L0_ffn_w_freq_real, dtype=f))
    wi = np.ascontiguousarray(np.asarray(L0_ffn_w_freq_imag, dtype=f))
    bias = np.ascontiguousarray(np.asarray(L0_ffn_bias, dtype=f).reshape(1, C))
    x = np.asarray(L0_attn_x, dtype=f)
    in_maps = []
    for c in range(n_cores):
        xs = np.ascontiguousarray(x[c * n_seq:(c + 1) * n_seq, :TT, :])
        in_maps.append({
            "x": xs, "wqt": wqt, "wkvt": wkvt, "wpt": wpt,
            "wr": wr, "wi": wi, "bias": bias, "cos": cos, "sinp": sinp,
        })
    return in_maps


def kernel(L0_attn_x, L0_attn_Wq, L0_attn_Wkv, L0_attn_Wproj,
           L0_ffn_w_freq_real, L0_ffn_w_freq_imag, L0_ffn_bias):
    in_maps = make_in_maps(L0_attn_x, L0_attn_Wq, L0_attn_Wkv, L0_attn_Wproj,
                           L0_ffn_w_freq_real, L0_ffn_w_freq_imag, L0_ffn_bias)
    nc = build_program()
    res = run_bass_kernel_spmd(nc, in_maps, core_ids=list(range(N_CORES)))
    out = np.concatenate([r["out"].reshape(N_SEQ, T, C) for r in res.results],
                         axis=0)
    return np.ascontiguousarray(out.astype(np.float32))


# version 11
# speedup vs baseline: 1.0454x; 1.0454x over previous
import dataclasses
import math
from contextlib import ExitStack

import numpy as np

import concourse.bass as bass
import concourse.mybir as mybir
from concourse.bass_utils import run_bass_kernel_spmd

F32 = mybir.dt.float32
F16 = mybir.dt.float16
I32 = mybir.dt.int32
AF = mybir.ActivationFunctionType
ALU = mybir.AluOpType

N_CORES = 8
B, T, C = 16, 2048, 1024
H, KV, HD = 16, 4, 64
N_SEQ = B // N_CORES
N_TB = T // 128
NCH = C // 128
EPS_LN = 1e-5
EPS_QK = 1e-6
SC = 1.0 / math.sqrt(HD)
SHI = np.float32(2 * np.pi / 1024.0)
def _slo():
    d = np.float64(np.pi) / 512 - np.float64(SHI)
    lo = np.float32(d)
    while 512.0 * (np.float64(SHI) + np.float64(lo)) > np.float64(np.pi):
        lo = np.nextafter(lo, np.float32(-np.inf))
    return lo
SLO = _slo()


def ap3(t, off, dims):
    base = t[:]
    return dataclasses.replace(base, offset=base.offset + off,
                               ap=[base.ap[0]] + [list(d) for d in dims])


def apP(t, p0, np_, off, dims):
    base = t[p0:p0 + np_]
    return dataclasses.replace(base, offset=base.offset + off,
                               ap=[base.ap[0]] + [list(d) for d in dims])


class Buf:
    __slots__ = ("name", "writer", "readers")

    def __init__(self, name):
        self.name = name
        self.writer = {}
        self.readers = {}


class Ctx:
    def __init__(self, nc, es):
        self.nc = nc
        self.es = es
        self.eng = {"t": nc.tensor, "v": nc.vector, "s": nc.scalar,
                    "g": nc.gpsimd, "y": nc.sync}
        self.sems = {}
        self.count = {}
        self.waited = {k: {} for k in ("t", "v", "s", "g", "y")}
        qnames = ["qw"] + [f"qx{i}" for i in range(10)] + [f"qo{i}" for i in range(10)]
        for name in ["t", "v", "s", "g"] + qnames:
            self.sems[name] = es.enter_context(nc.semaphore("sem_" + name))
            self.count[name] = 0
        self.qeng = {"qw": "g"}
        for i in range(10):
            self.qeng[f"qx{i}"] = "y"
            self.qeng[f"qo{i}"] = "s"
        self.fencetok = {}

    def fence(self):
        self.fencetok = {s: v for s, v in self.count.items() if v > 0}

    def _waits(self, ename, need):
        w = self.waited[ename]
        for s in sorted(need):
            v = need[s]
            if s == ename and v > self.count[s]:
                continue
            if v > w.get(s, 0):
                self.eng[ename].wait_ge(self.sems[s], v)
                w[s] = v

    def _collect(self, reads, writes):
        need = dict(self.fencetok)
        for b in reads:
            for s, v in b.writer.items():
                if v > need.get(s, 0):
                    need[s] = v
        for b in writes:
            for d in (b.writer, b.readers):
                for s, v in d.items():
                    if v > need.get(s, 0):
                        need[s] = v
        return need

    def op(self, ename, fn, reads=(), writes=(), inc=1):
        self._waits(ename, self._collect(reads, writes))
        inst = fn(self.eng[ename])
        if inc:
            self.count[ename] += 1
            inst.then_inc(self.sems[ename], 1)
        tok = self.count[ename] + (0 if inc else 1)
        for b in writes:
            b.writer = {ename: tok}
            b.readers = {}
        for b in reads:
            if b.readers.get(ename, 0) < tok:
                b.readers[ename] = tok
        return inst

    def dma(self, q, fn, reads=(), writes=()):
        ename = self.qeng[q]
        need = self._collect(reads, writes)
        if self.count[q] > need.get(q, 0):
            need[q] = self.count[q]
        self._waits(ename, need)
        inst = fn(self.eng[ename])
        self.count[q] += 16
        inst.then_inc(self.sems[q], 16)
        tok = self.count[q]
        for b in writes:
            b.writer = {q: tok}
            b.readers = {}
        for b in reads:
            if b.readers.get(q, 0) < tok:
                b.readers[q] = tok

    def sb(self, name, shape, dtype):
        t = self.es.enter_context(self.nc.sbuf_tensor(name, shape, dtype))
        return t, Buf(name)

    def ps(self, name, shape, dtype=F32):
        t = self.es.enter_context(self.nc.psum_tensor(name, shape, dtype))
        return t, Buf(name)


def build_program(n_seq=N_SEQ, n_tb=N_TB):
    TT = n_tb * 128
    nc = bass.Bass("TRN2", target_bir_lowering=False, debug=False)

    d_x = nc.dram_tensor("x", [n_seq, TT, C], F32, kind="ExternalInput")
    d_wqt = nc.dram_tensor("wqt", [C, C], F32, kind="ExternalInput")
    d_wkvt = nc.dram_tensor("wkvt", [C, 512], F32, kind="ExternalInput")
    d_wpt = nc.dram_tensor("wpt", [C, C], F32, kind="ExternalInput")
    d_wr = nc.dram_tensor("wr", [C, C], F32, kind="ExternalInput")
    d_wi = nc.dram_tensor("wi", [C, C], F32, kind="ExternalInput")
    d_bias = nc.dram_tensor("bias", [1, C], F32, kind="ExternalInput")
    d_cos = nc.dram_tensor("cos", [TT, HD], F32, kind="ExternalInput")
    d_sinp = nc.dram_tensor("sinp", [TT, HD], F32, kind="ExternalInput")
    d_out = nc.dram_tensor("out", [n_seq, TT, C], F32, kind="ExternalOutput")

    es = ExitStack()
    cx = Ctx(nc, es)

    wq, b_wq = cx.sb("wq16", [128, NCH * C], F16)
    wkv, b_wkv = cx.sb("wkv16", [128, NCH * 512], F16)
    wp, b_wp = cx.sb("wp16", [128, NCH * C], F16)
    mx, b_mx = cx.sb("mx16", [128, NCH * C], F16)
    cost, b_cos = cx.sb("cos16", [128, n_tb * HD], F16)
    sint, b_sin = cx.sb("sinp16", [128, n_tb * HD], F16)
    kt = [None, None]
    b_ktslots = [[Buf(f"kts{r}_{i}") for i in range(n_tb)] for r in range(2)]
    for reg in range(2):
        kt[reg] = es.enter_context(nc.sbuf_tensor(f"ktil{reg}", [128, TT], F16))
    vpl, _ = cx.sb("vplus", [128, n_tb * 260], F16)
    b_vpslots = [Buf(f"vps{i}") for i in range(n_tb)]
    bias16, b_bias = cx.sb("bias16", [1, C], F16)
    ones1, b_ones1 = cx.sb("ones1", [1, 128], F16)
    ident, b_ident = cx.sb("ident", [128, 128], F16)
    m0, b_m0 = cx.sb("mask0", [128, 128], F16)
    m4, b_m4 = cx.sb("mask4", [128, 128], F16)

    sring, _ = cx.ps("sring", [128, 3 * 512])
    b_slot = [Buf(f"slot{i}") for i in range(3)]
    avps, _ = cx.ps("avps", [128, 1024])
    b_avbank = [Buf("avb0"), Buf("avb1")]
    ptps, b_pt = cx.ps("ptps", [128, 1024], F16)
    prj, _ = cx.ps("prj", [128, 1024])
    b_prjb = [Buf("prj0"), Buf("prj1")]

    def t_(fn, reads=(), writes=(), inc=1):
        return cx.op("t", fn, reads, writes, inc)

    def v_(fn, reads=(), writes=(), inc=1):
        return cx.op("v", fn, reads, writes, inc)

    def s_(fn, reads=(), writes=(), inc=1):
        return cx.op("s", fn, reads, writes, inc)

    def g_(fn, reads=(), writes=(), inc=1):
        return cx.op("g", fn, reads, writes, inc)

    pro_es = ExitStack()
    kio, b_kio = (pro_es.enter_context(nc.sbuf_tensor("kio", [128, C], F32)), Buf("kio"))
    jcol, b_jcol = (pro_es.enter_context(nc.sbuf_tensor("jcol", [128, 1], F32)), Buf("jcol"))
    g_(lambda e: e.iota(kio[:], pattern=[[1, C]], base=0, channel_multiplier=0,
                        allow_small_or_imprecise_dtypes=True), writes=[b_kio])
    g_(lambda e: e.iota(jcol[:], pattern=[[0, 1]], base=0, channel_multiplier=1,
                        allow_small_or_imprecise_dtypes=True), writes=[b_jcol])
    v_(lambda e: e.tensor_scalar(out=ident[:], in0=kio[:, 0:128], scalar1=jcol[:],
                                 scalar2=None, op0=ALU.is_equal),
       reads=[b_kio, b_jcol], writes=[b_ident])
    v_(lambda e: e.tensor_scalar(out=m0[:], in0=kio[:, 0:128], scalar1=jcol[:],
                                 scalar2=None, op0=ALU.is_ge),
       reads=[b_kio, b_jcol], writes=[b_m0])
    v_(lambda e: e.tensor_scalar(out=m4[:], in0=kio[:, 0:128], scalar1=jcol[:],
                                 scalar2=None, op0=ALU.is_lt),
       reads=[b_kio, b_jcol], writes=[b_m4])
    v_(lambda e: e.memset(ones1[:], 1.0), writes=[b_ones1])

    cx.dma("qw", lambda e: e.dma_start(
        out=ap3(wq, 0, [[C, NCH], [1, C]]),
        in_=d_wqt.ap().rearrange("(cc p) i -> p cc i", p=128)), writes=[b_wq])
    cx.dma("qw", lambda e: e.dma_start(
        out=ap3(wkv, 0, [[512, NCH], [1, 512]]),
        in_=d_wkvt.ap().rearrange("(cc p) j -> p cc j", p=128)), writes=[b_wkv])
    cx.dma("qw", lambda e: e.dma_start(
        out=ap3(wp, 0, [[C, NCH], [1, C]]),
        in_=d_wpt.ap().rearrange("(ic p) o -> p ic o", p=128)), writes=[b_wp])
    cx.dma("qw", lambda e: e.dma_start(
        out=ap3(cost, 0, [[HD, n_tb], [1, HD]]),
        in_=d_cos.ap().rearrange("(tb p) d -> p tb d", p=128)), writes=[b_cos])
    cx.dma("qw", lambda e: e.dma_start(
        out=ap3(sint, 0, [[HD, n_tb], [1, HD]]),
        in_=d_sinp.ap().rearrange("(tb p) d -> p tb d", p=128)), writes=[b_sin])
    cx.dma("qw", lambda e: e.dma_start(out=bias16[:], in_=d_bias.ap()),
           writes=[b_bias])

    with ExitStack() as mes:
        def msb(nm, sh, dt):
            return mes.enter_context(nc.sbuf_tensor(nm, sh, dt)), Buf(nm)

        c16, b_c16 = msb("c16", [128, NCH * C], F16)
        s16, b_s16 = msb("s16", [128, NCH * C], F16)
        wr16, b_wr16 = msb("wr16", [128, NCH * C], F16)
        wi16, b_wi16 = msb("wi16", [128, NCH * C], F16)
        pt16, b_pt16 = msb("ptm16", [128, NCH * C], F16)
        qt16, b_qt16 = msb("qtm16", [128, NCH * C], F16)
        jk, b_jk = msb("jk", [128, C], F32)
        red, b_red = msb("red", [128, C], F32)
        scr, b_scr = msb("mscr", [128, C], F32)
        qi, b_qi = msb("qi", [128, C], I32)
        ang, b_ang = msb("angm", [128, C], F32)

        cx.dma("qw", lambda e: e.dma_start(
            out=ap3(wr16, 0, [[C, NCH], [1, C]]),
            in_=d_wr.ap().rearrange("(kc p) o -> p kc o", p=128)), writes=[b_wr16])
        cx.dma("qw", lambda e: e.dma_start(
            out=ap3(wi16, 0, [[C, NCH], [1, C]]),
            in_=d_wi.ap().rearrange("(kc p) o -> p kc o", p=128)), writes=[b_wi16])

        for cc in range(NCH):
            v_(lambda e, cc=cc: e.tensor_scalar(
                out=scr[:, 0:1], in0=jcol[:], scalar1=float(128 * cc),
                scalar2=None, op0=ALU.add), reads=[b_jcol], writes=[b_scr])
            v_(lambda e: e.tensor_scalar(
                out=jk[:], in0=kio[:], scalar1=scr[:, 0:1], scalar2=None,
                op0=ALU.mult), reads=[b_kio, b_scr], writes=[b_jk])
            v_(lambda e: e.tensor_scalar(
                out=qi[:], in0=jk[:], scalar1=float(2.0 ** -10),
                scalar2=float(0.5 + 2.0 ** -12), op0=ALU.mult, op1=ALU.add),
               reads=[b_jk], writes=[b_qi])
            v_(lambda e: e.tensor_copy(out=red[:], in_=qi[:]),
               reads=[b_qi], writes=[b_red])
            v_(lambda e: e.scalar_tensor_tensor(
                out=red[:], in0=red[:], scalar=-1024.0, in1=jk[:],
                op0=ALU.mult, op1=ALU.add), reads=[b_red, b_jk], writes=[b_red])
            v_(lambda e: e.tensor_single_scalar(
                out=scr[:], in_=red[:], scalar=512.0, op=ALU.is_ge),
               reads=[b_red], writes=[b_scr])
            v_(lambda e: e.scalar_tensor_tensor(
                out=red[:], in0=scr[:], scalar=-1024.0, in1=red[:],
                op0=ALU.mult, op1=ALU.add), reads=[b_scr, b_red], writes=[b_red])
            v_(lambda e: e.tensor_single_scalar(
                out=scr[:], in_=red[:], scalar=-512.0, op=ALU.is_lt),
               reads=[b_red], writes=[b_scr])
            v_(lambda e: e.scalar_tensor_tensor(
                out=red[:], in0=scr[:], scalar=1024.0, in1=red[:],
                op0=ALU.mult, op1=ALU.add), reads=[b_scr, b_red], writes=[b_red])
            v_(lambda e: e.tensor_scalar(
                out=ang[:], in0=red[:], scalar1=float(SHI), scalar2=None,
                op0=ALU.mult), reads=[b_red], writes=[b_ang])
            v_(lambda e: e.scalar_tensor_tensor(
                out=ang[:], in0=red[:], scalar=float(SLO), in1=ang[:],
                op0=ALU.mult, op1=ALU.add), reads=[b_red, b_ang], writes=[b_ang])
            s_(lambda e, cc=cc: e.activation(
                out=s16[:, cc * C:(cc + 1) * C], in_=ang[:], func=AF.Sin),
               reads=[b_ang], writes=[b_s16])
            v_(lambda e: e.tensor_single_scalar(
                out=scr[:], in_=red[:], scalar=256.0, op=ALU.is_ge),
               reads=[b_red], writes=[b_scr])
            v_(lambda e: e.scalar_tensor_tensor(
                out=red[:], in0=scr[:], scalar=-1024.0, in1=red[:],
                op0=ALU.mult, op1=ALU.add), reads=[b_scr, b_red], writes=[b_red])
            v_(lambda e: e.tensor_scalar(
                out=red[:], in0=red[:], scalar1=256.0, scalar2=None,
                op0=ALU.add), reads=[b_red], writes=[b_red])
            v_(lambda e: e.tensor_scalar(
                out=ang[:], in0=red[:], scalar1=float(SHI), scalar2=None,
                op0=ALU.mult), reads=[b_red], writes=[b_ang])
            v_(lambda e: e.scalar_tensor_tensor(
                out=ang[:], in0=red[:], scalar=float(SLO), in1=ang[:],
                op0=ALU.mult, op1=ALU.add), reads=[b_red, b_ang], writes=[b_ang])
            s_(lambda e, cc=cc: e.activation(
                out=c16[:, cc * C:(cc + 1) * C], in_=ang[:], func=AF.Sin),
               reads=[b_ang], writes=[b_c16])
        def negate_wi():
            v_(lambda e: e.tensor_scalar(out=wi16[:], in0=wi16[:], scalar1=-1.0,
                                         scalar2=None, op0=ALU.mult),
               reads=[b_wi16], writes=[b_wi16])
        for dst16, b_dst, terms in (
            (pt16, b_pt16, [(wr16, b_wr16, c16, b_c16), (wi16, b_wi16, s16, b_s16)]),
            (qt16, b_qt16, [(wi16, b_wi16, c16, b_c16), (wr16, b_wr16, s16, b_s16)]),
        ):
            if dst16 is qt16:
                negate_wi()
            for oc in range(NCH):
                for ns in range(2):
                    k_i = 0
                    nmm = 2 * NCH
                    for (wsb, b_w, msb_, b_m) in terms:
                        for kc in range(NCH):
                            last = k_i == nmm - 1
                            t_(lambda e, oc=oc, ns=ns, kc=kc, wsb=wsb, msb_=msb_,
                               first=(k_i == 0), last=last: e.matmul(
                                   prj[:, ns * 512:(ns + 1) * 512],
                                   wsb[:, kc * C + oc * 128: kc * C + oc * 128 + 128],
                                   msb_[:, kc * C + ns * 512: kc * C + (ns + 1) * 512],
                                   start=first, stop=last),
                               reads=[b_w, b_m], writes=[b_prjb[ns]],
                               inc=1 if last else 0)
                            k_i += 1
                    s_(lambda e, oc=oc, ns=ns, dst16=dst16: e.activation(
                        out=dst16[:, oc * C + ns * 512: oc * C + (ns + 1) * 512],
                        in_=prj[:, ns * 512:(ns + 1) * 512], func=AF.Copy,
                        scale=float(2.0 ** -10)),
                       reads=[b_prjb[ns]], writes=[b_dst])
        for jc in range(NCH):
            for ns in range(2):
                k_i = 0
                for (lsb, b_l, msb_, b_m) in ((pt16, b_pt16, c16, b_c16),
                                              (qt16, b_qt16, s16, b_s16)):
                    for oc in range(NCH):
                        last = k_i == 2 * NCH - 1
                        t_(lambda e, jc=jc, ns=ns, oc=oc, lsb=lsb, msb_=msb_,
                           first=(k_i == 0), last=last: e.matmul(
                               prj[:, ns * 512:(ns + 1) * 512],
                               lsb[:, oc * C + jc * 128: oc * C + jc * 128 + 128],
                               msb_[:, oc * C + ns * 512: oc * C + (ns + 1) * 512],
                               start=first, stop=last),
                           reads=[b_l, b_m], writes=[b_prjb[ns]],
                           inc=1 if last else 0)
                        k_i += 1
                s_(lambda e, jc=jc, ns=ns: e.activation(
                    out=mx[:, jc * C + ns * 512: jc * C + (ns + 1) * 512],
                    in_=prj[:, ns * 512:(ns + 1) * 512], func=AF.Copy),
                   reads=[b_prjb[ns]], writes=[b_mx])
    pro_es.close()
    cx.fence()

    NXB = 10
    xb, b_xb = zip(*[cx.sb(f"xb{i}", [128, C], F32) for i in range(NXB)])
    lnT, b_lnT = zip(*[cx.sb(f"lnT{i}", [128, C], F16) for i in range(2)])
    ln2T, b_ln2T = zip(*[cx.sb(f"ln2T{i}", [128, C], F16) for i in range(2)])
    lntm, b_lntm = zip(*[cx.sb(f"lntm{i}", [128, C], F16) for i in range(4)])
    ln2tm, b_ln2tm = zip(*[cx.sb(f"ln2tm{i}", [128, C], F16) for i in range(3)])
    qtm, b_qtm = zip(*[cx.sb(f"qtm{i}", [128, C], F16) for i in range(3)])
    qtm2, b_qtm2 = zip(*[cx.sb(f"qtm2_{i}", [128, C], F16) for i in range(3)])
    qfm, b_qfm = zip(*[cx.sb(f"qfm{i}", [128, C], F16) for i in range(3)])
    ktm, b_ktm = zip(*[cx.sb(f"ktm{i}", [128, 256], F16) for i in range(3)])
    ktm2, b_ktm2 = zip(*[cx.sb(f"ktm2_{i}", [128, 256], F16) for i in range(3)])
    sscr, b_sscr = zip(*[cx.sb(f"sscr{i}", [128, C], F32) for i in range(2)])
    sdum, b_sdum = cx.sb("sdum", [128, C], F32)
    exs, b_exs = zip(*[cx.sb(f"exs{i}", [128, 2 * 2560], F16) for i in range(2)])
    b_exsg = [[Buf(f"exsg{i}_{g}") for g in range(2)] for i in range(2)]
    av16, b_av16 = zip(*[cx.sb(f"av16_{i}", [128, C], F16) for i in range(3)])
    avfm, b_avfm = zip(*[cx.sb(f"avfm{i}", [128, C], F16) for i in range(2)])
    st_, b_st = zip(*[cx.sb(f"stats{i}", [128, 16], F32) for i in range(8)])
    qss, b_qss = zip(*[cx.sb(f"qss{i}", [128, 24], F32) for i in range(2)])
    rcq, b_rcq = zip(*[cx.sb(f"rcq{i}", [128, 24], F32) for i in range(2)])
    rdn, b_rdn = zip(*[cx.sb(f"rdn{i}", [128, 16], F32) for i in range(2)])


    n_ticks = n_seq * n_tb

    def x_load(tau):
        s, i = divmod(tau, n_tb)
        xi = tau % NXB
        cx.dma(f"qx{xi}", lambda e: e.dma_start(
            out=xb[xi][:], in_=d_x.ap()[s, i * 128:(i + 1) * 128, :]),
            writes=[b_xb[xi]])


    def ln_stats(xbuf, b_xbuf, scrt, b_scrt, sti):
        stt, b_stt = st_[sti], b_st[sti]
        s_(lambda e: e.activation(out=scrt[:], in_=xbuf[:], func=AF.Square,
                                  accum_out=stt[:, 0:1]),
           reads=[b_xbuf], writes=[b_scrt, b_stt])
        s_(lambda e: e.activation(out=scrt[:], in_=xbuf[:], func=AF.Copy,
                                  accum_out=stt[:, 1:2]),
           reads=[b_xbuf], writes=[b_scrt, b_stt])
        s_(lambda e: e.activation(out=stt[:, 2:3], in_=stt[:, 1:2], func=AF.Copy,
                                  scale=float(2.0 ** -10)),
           reads=[b_stt], writes=[b_stt])
        s_(lambda e: e.activation(out=stt[:, 3:4], in_=stt[:, 2:3],
                                  func=AF.Square),
           reads=[b_stt], writes=[b_stt])
        s_(lambda e: e.activation(out=stt[:, 4:5], in_=stt[:, 3:4], func=AF.Copy,
                                  scale=-1.0, bias=float(EPS_LN)),
           reads=[b_stt], writes=[b_stt])
        s_(lambda e: e.activation(out=stt[:, 6:7], in_=stt[:, 0:1], func=AF.Ln,
                                  scale=float(2.0 ** -10), bias=stt[:, 4:5]),
           reads=[b_stt], writes=[b_stt])
        s_(lambda e: e.activation(out=stt[:, 7:8], in_=stt[:, 6:7], func=AF.Exp,
                                  scale=-0.5), reads=[b_stt], writes=[b_stt])
        s_(lambda e: e.activation(out=stt[:, 9:10], in_=stt[:, 2:3], func=AF.Copy,
                                  scale=stt[:, 7:8]),
           reads=[b_stt], writes=[b_stt])

    def ln_apply(xbuf, b_xbuf, dst, b_dst, sti):
        stt, b_stt = st_[sti], b_st[sti]
        v_(lambda e: e.tensor_scalar(out=dst[:], in0=xbuf[:], scalar1=stt[:, 7:8],
                                     scalar2=stt[:, 9:10], op0=ALU.mult,
                                     op1=ALU.subtract),
           reads=[b_xbuf, b_stt], writes=[b_dst])

    def transpose8(srct, b_src, dstt, b_dst, nch=NCH):
        for ccc in range(nch):
            t_(lambda e, ccc=ccc: e.transpose(
                ptps[:, ccc * 128:(ccc + 1) * 128],
                srct[:, ccc * 128:(ccc + 1) * 128], ident[:]),
               reads=[b_src, b_ident], writes=[b_pt],
               inc=1 if ccc == nch - 1 else 0)
        v_(lambda e: e.tensor_copy(out=dstt[:, 0:nch * 128],
                                   in_=ptps[:, 0:nch * 128]),
           reads=[b_pt], writes=[b_dst])

    def stage_ln2(ptau):
        pr = ptau % 2
        q3 = ptau % 3
        pxi = ptau % NXB
        ln_stats(xb[pxi], b_xb[pxi], sdum, b_sdum, 4 + pr)
        ln_apply(xb[pxi], b_xb[pxi], ln2tm[q3], b_ln2tm[q3], 4 + pr)

    def stage_ffn_out(ptau):
        pr = ptau % 2
        q3 = ptau % 3
        ps_, pi = divmod(ptau, n_tb)
        pxi = ptau % NXB
        transpose8(ln2tm[q3], b_ln2tm[q3], ln2T[pr], b_ln2T[pr])
        for ns in range(2):
            for ccc in range(NCH):
                t_(lambda e, ccc=ccc, ns=ns, pr=pr: e.matmul(
                    prj[:, ns * 512:(ns + 1) * 512],
                    ln2T[pr][:, ccc * 128:(ccc + 1) * 128],
                    mx[:, ccc * C + ns * 512: ccc * C + (ns + 1) * 512],
                    start=(ccc == 0), stop=False),
                   reads=[b_ln2T[pr], b_mx], writes=[b_prjb[ns]], inc=0)
            t_(lambda e, ns=ns: e.matmul(
                prj[:, ns * 512:(ns + 1) * 512], ones1[:],
                bias16[:, ns * 512:(ns + 1) * 512], start=False, stop=True),
               reads=[b_ones1, b_bias], writes=[b_prjb[ns]], inc=1)
            v_(lambda e, ns=ns, pxi=pxi: e.tensor_tensor(
                out=xb[pxi][:, ns * 512:(ns + 1) * 512],
                in0=prj[:, ns * 512:(ns + 1) * 512],
                in1=xb[pxi][:, ns * 512:(ns + 1) * 512], op=ALU.add),
               reads=[b_prjb[ns], b_xb[pxi]], writes=[b_xb[pxi]])
        cx.dma(f"qo{pxi}", lambda e, ps_=ps_, pi=pi, pxi=pxi: e.dma_start(
            out=d_out.ap()[ps_, pi * 128:(pi + 1) * 128, :], in_=xb[pxi][:]),
            reads=[b_xb[pxi]])

    slot_ctr = [0]

    def stage_ln1t_pe(tau):
        for ccc in range(NCH):
            t_(lambda e, ccc=ccc, tau=tau: e.transpose(
                ptps[:, ccc * 128:(ccc + 1) * 128],
                lntm[tau % 4][:, ccc * 128:(ccc + 1) * 128], ident[:]),
               reads=[b_lntm[tau % 4], b_ident], writes=[b_pt],
               inc=1 if ccc == NCH - 1 else 0)

    def stage_ln1t_evac(tau):
        r = tau % 2
        v_(lambda e, r=r: e.tensor_copy(out=lnT[r][:, 0:C], in_=ptps[:, 0:C]),
           reads=[b_pt], writes=[b_lnT[r]])

    def stage_qkv(tau):
        s, i = divmod(tau, n_tb)
        r = tau % 2
        q3 = tau % 3
        xi = tau % NXB
        for ns in range(2):
            for ccc in range(NCH):
                t_(lambda e, ns=ns, ccc=ccc, r=r: e.matmul(
                    prj[:, ns * 512:(ns + 1) * 512],
                    lnT[r][:, ccc * 128:(ccc + 1) * 128],
                    wq[:, ccc * C + ns * 512: ccc * C + (ns + 1) * 512],
                    start=(ccc == 0), stop=(ccc == NCH - 1)),
                   reads=[b_lnT[r], b_wq], writes=[b_prjb[ns]],
                   inc=1 if ccc == NCH - 1 else 0)
            v_(lambda e, ns=ns, r=r: e.tensor_copy(
                out=qtm[q3][:, ns * 512:(ns + 1) * 512],
                in_=prj[:, ns * 512:(ns + 1) * 512]),
               reads=[b_prjb[ns]], writes=[b_qtm[q3]])
        for ccc in range(NCH):
            t_(lambda e, ccc=ccc, r=r: e.matmul(
                prj[:, 0:512], lnT[r][:, ccc * 128:(ccc + 1) * 128],
                wkv[:, ccc * 512:(ccc + 1) * 512],
                start=(ccc == 0), stop=(ccc == NCH - 1)),
               reads=[b_lnT[r], b_wkv], writes=[b_prjb[0]],
               inc=1 if ccc == NCH - 1 else 0)
        v_(lambda e, r=r: e.tensor_copy(out=ktm[q3][:, 0:256], in_=prj[:, 0:256]),
           reads=[b_prjb[0]], writes=[b_ktm[q3]])
        v_(lambda e, i=i: e.tensor_copy(
            out=ap3(vpl, i * 260, [[65, 4], [1, 64]]),
            in_=dataclasses.replace(prj[:], offset=prj[:].offset + 256,
                                    ap=[prj[:].ap[0], [64, 4], [1, 64]])),
           reads=[b_prjb[0]], writes=[b_vpslots[i]])
        v_(lambda e, i=i: e.memset(ap3(vpl, i * 260 + 64, [[65, 4], [1, 1]]), 1.0),
           writes=[b_vpslots[i]])


    def stage_norm_rope(tau):
        s, i = divmod(tau, n_tb)
        r = tau % 2
        q3 = tau % 3
        qs, b_qs = qss[r], b_qss[r]
        rc, b_rc = rcq[r], b_rcq[r]
        sc2, b_sc2 = sscr[r], b_sscr[r]
        s_(lambda e, r=r: e.activation(out=sc2[:], in_=qtm[q3][:], func=AF.Square),
           reads=[b_qtm[q3]], writes=[b_sc2])
        v_(lambda e, r=r: e.tensor_reduce(
            out=ap3(qss[r], 0, [[1, 16], [1, 1]]),
            in_=ap3(sscr[r], 0, [[64, 16], [1, 64]]),
            axis=mybir.AxisListType.X, op=ALU.add),
           reads=[b_sc2], writes=[b_qs])
        s_(lambda e, r=r: e.activation(out=sc2[:, 0:256], in_=ktm[q3][:, 0:256],
                                       func=AF.Square),
           reads=[b_ktm[q3]], writes=[b_sc2])
        v_(lambda e, r=r: e.tensor_reduce(
            out=ap3(qss[r], 16, [[1, 4], [1, 1]]),
            in_=ap3(sscr[r], 0, [[64, 4], [1, 64]]),
            axis=mybir.AxisListType.X, op=ALU.add),
           reads=[b_sc2], writes=[b_qs])
        s_(lambda e, r=r: e.activation(out=ap3(qss[r], 0, [[1, 20]]),
                                       in_=ap3(qss[r], 0, [[1, 20]]),
                                       func=AF.Sqrt),
           reads=[b_qs], writes=[b_qs])
        v_(lambda e, r=r: e.tensor_scalar(out=ap3(qss[r], 0, [[1, 20]]),
                                          in0=ap3(qss[r], 0, [[1, 20]]),
                                          scalar1=float(EPS_QK), scalar2=None,
                                          op0=ALU.add), reads=[b_qs], writes=[b_qs])
        v_(lambda e, r=r: e.reciprocal(out=ap3(rcq[r], 0, [[1, 20]]),
                                       in_=ap3(qss[r], 0, [[1, 20]])),
           reads=[b_qs], writes=[b_rc])
        v_(lambda e, r=r: e.tensor_scalar(out=ap3(rcq[r], 0, [[1, 16]]),
                                          in0=ap3(rcq[r], 0, [[1, 16]]),
                                          scalar1=float(SC), scalar2=None,
                                          op0=ALU.mult), reads=[b_rc], writes=[b_rc])
        for (srcb, b_srcb, dstb, b_dstb, nh, roff) in (
                (qtm[q3], b_qtm[q3], qtm2[q3], b_qtm2[q3], 16, 0),
                (ktm[q3], b_ktm[q3], ktm2[q3], b_ktm2[q3], 4, 16)):
            v_(lambda e, srcb=srcb, dstb=dstb, nh=nh, roff=roff: e.tensor_tensor(
                out=ap3(dstb, 0, [[64, nh], [1, 64]]),
                in0=ap3(srcb, 0, [[64, nh], [1, 64]]),
                in1=ap3(rc, roff, [[1, nh], [0, 64]]), op=ALU.mult),
               reads=[b_srcb, b_rc], writes=[b_dstb])
            v_(lambda e, srcb=srcb, dstb=dstb, nh=nh, i=i: e.tensor_tensor(
                out=ap3(srcb, 0, [[64, nh], [32, 2], [1, 32]]),
                in0=dataclasses.replace(
                    dstb[:], offset=dstb[:].offset + 32,
                    ap=[dstb[:].ap[0], [64, nh], [-32, 2], [1, 32]]),
                in1=ap3(sint, i * HD, [[0, nh], [32, 2], [1, 32]]),
                op=ALU.mult),
               reads=[b_dstb, b_sin], writes=[b_srcb])
            v_(lambda e, dstb=dstb, nh=nh, i=i: e.tensor_tensor(
                out=ap3(dstb, 0, [[64, nh], [1, 64]]),
                in0=ap3(dstb, 0, [[64, nh], [1, 64]]),
                in1=ap3(cost, i * HD, [[0, nh], [1, 64]]), op=ALU.mult),
               reads=[b_dstb, b_cos], writes=[b_dstb])
            v_(lambda e, srcb=srcb, dstb=dstb, nh=nh: e.tensor_tensor(
                out=ap3(dstb, 0, [[64, nh], [1, 64]]),
                in0=ap3(dstb, 0, [[64, nh], [1, 64]]),
                in1=ap3(srcb, 0, [[64, nh], [1, 64]]), op=ALU.add),
               reads=[b_dstb, b_srcb], writes=[b_dstb])


    def stage_pre_b(tau):
        s, i = divmod(tau, n_tb)
        r = tau % 2
        q3 = tau % 3
        for h in range(H):
            rowb = ((h // 4) % 2) * 64
            strip = (h % 4) + 4 * (h // 8)
            t_(lambda e, h=h, rowb=rowb, strip=strip, r=r: e.transpose(
                dataclasses.replace(ptps[rowb:rowb + 64],
                                    offset=ptps[rowb:rowb + 64].offset + strip * 128,
                                    ap=[ptps[rowb:rowb + 64].ap[0], [1, 128]]),
                qtm2[q3][:, h * 64:(h + 1) * 64], ident[:]),
               reads=[b_qtm2[q3], b_ident], writes=[b_pt],
               inc=1 if h == H - 1 else 0)
        v_(lambda e, r=r: e.tensor_copy(out=qfm[q3][:, 0:C], in_=ptps[:, 0:C]),
           reads=[b_pt], writes=[b_qfm[q3]])
        for reg in range(2):
            t_(lambda e, reg=reg, r=r: e.transpose(
                ptps[:, reg * 128:(reg + 1) * 128],
                ktm2[q3][:, reg * 128:(reg + 1) * 128], ident[:]),
               reads=[b_ktm2[q3], b_ident], writes=[b_pt],
               inc=1 if reg == 1 else 0)
        for reg in range(2):
            v_(lambda e, reg=reg, i=i: e.tensor_copy(
                out=kt[reg][:, i * 128:(i + 1) * 128],
                in_=ptps[:, reg * 128:(reg + 1) * 128]),
               reads=[b_pt], writes=[b_ktslots[reg][i]])

    def stage_att(tau):
        s, i = divmod(tau, n_tb)
        r = tau % 2
        q3 = tau % 3
        xi = tau % NXB
        kcs = list(range(max(0, i - 4), i + 1))
        nkc = len(kcs)
        exsb, b_exsb = exs[r], b_exs[r]

        def emit_scores(g):
            reg, row = g // 2, (g % 2) * 64
            pend = []
            for kidx, kc in enumerate(kcs):
                sl = slot_ctr[0] % 3
                slot_ctr[0] += 1
                for j in range(4):
                    h = 4 * g + j
                    strip = (h % 4) + 4 * (h // 8)
                    t_(lambda e, sl=sl, j=j, reg=reg, row=row, kc=kc,
                       strip=strip, r=r, g=g: e.matmul(
                           sring[:, sl * 512 + j * 128: sl * 512 + (j + 1) * 128],
                           apP(kt[reg], row, 64, kc * 128, [[1, 128]]),
                           apP(qfm[q3], row, 64, strip * 128, [[1, 128]]),
                           start=(j == 0), stop=(j == 3)),
                       reads=[b_ktslots[reg][kc], b_qfm[q3]], writes=[b_slot[sl]],
                       inc=1 if j == 3 else 0)
                pend.append((kidx, sl))
                while len(pend) >= 2:
                    if pend[1][1] == pend[0][1] + 1:
                        k0, s0 = pend[0]
                        del pend[0:2]
                        s_(lambda e, k0=k0, s0=s0, g=g: e.activation(
                            out=exsb[:, (g % 2) * 2560 + k0 * 512:
                                     (g % 2) * 2560 + (k0 + 2) * 512],
                            in_=sring[:, s0 * 512:(s0 + 2) * 512], func=AF.Exp),
                           reads=[b_slot[s0], b_slot[s0 + 1]],
                           writes=[b_exsg[r][g % 2]])
                    else:
                        k0, s0 = pend.pop(0)
                        s_(lambda e, k0=k0, s0=s0, g=g: e.activation(
                            out=exsb[:, (g % 2) * 2560 + k0 * 512:
                                     (g % 2) * 2560 + (k0 + 1) * 512],
                            in_=sring[:, s0 * 512:(s0 + 1) * 512], func=AF.Exp),
                           reads=[b_slot[s0]], writes=[b_exsg[r][g % 2]])
            for k0, s0 in pend:
                s_(lambda e, k0=k0, s0=s0, g=g: e.activation(
                    out=exsb[:, (g % 2) * 2560 + k0 * 512:
                             (g % 2) * 2560 + (k0 + 1) * 512],
                    in_=sring[:, s0 * 512:(s0 + 1) * 512], func=AF.Exp),
                   reads=[b_slot[s0]], writes=[b_exsg[r][g % 2]])
            pend.clear()
            v_(lambda e, kidx=nkc - 1, g=g: e.tensor_tensor(
                out=ap3(exsb, (g % 2) * 2560 + kidx * 512, [[128, 4], [1, 128]]),
                in0=ap3(exsb, (g % 2) * 2560 + kidx * 512, [[128, 4], [1, 128]]),
                in1=ap3(m0, 0, [[0, 4], [1, 128]]), op=ALU.mult),
               reads=[b_exsg[r][g % 2], b_m0], writes=[b_exsg[r][g % 2]])
            if i >= 4:
                v_(lambda e, g=g: e.tensor_tensor(
                    out=ap3(exsb, (g % 2) * 2560, [[128, 4], [1, 128]]),
                    in0=ap3(exsb, (g % 2) * 2560, [[128, 4], [1, 128]]),
                    in1=ap3(m4, 0, [[0, 4], [1, 128]]), op=ALU.mult),
                   reads=[b_exsg[r][g % 2], b_m4], writes=[b_exsg[r][g % 2]])

        def emit_av(g):
            bank = g % 2
            for kidx, kc in enumerate(kcs):
                for j in range(4):
                    first = kidx == 0 and j == 0
                    last = kidx == nkc - 1 and j == 3
                    t_(lambda e, kidx=kidx, kc=kc, j=j, g=g, bank=bank,
                       first=first, last=last: e.matmul(
                           avps[:, bank * 512 + j * 128: bank * 512 + j * 128 + 65],
                           exsb[:, (g % 2) * 2560 + kidx * 512 + j * 128:
                                (g % 2) * 2560 + kidx * 512 + (j + 1) * 128],
                           ap3(vpl, kc * 260 + g * 65, [[1, 65]]),
                           start=first, stop=last),
                       reads=[b_exsg[r][g % 2], b_vpslots[kc]], writes=[b_avbank[bank]],
                       inc=1 if last else 0)
            if g % 2 == 1:
                hp = g // 2
                rd, b_rd = rdn[r], b_rdn[r]
                v_(lambda e, hp=hp, rd=rd: e.reciprocal(
                    out=ap3(rd, hp * 8, [[1, 8]]),
                    in_=dataclasses.replace(
                        avps[:], offset=avps[:].offset + 64,
                        ap=[avps[:].ap[0], [128, 8], [1, 1]])),
                   reads=[b_avbank[0], b_avbank[1]], writes=[b_rd])
                v_(lambda e, hp=hp, rd=rd, r=r: e.tensor_tensor(
                    out=ap3(av16[q3], hp * 512, [[64, 8], [1, 64]]),
                    in0=dataclasses.replace(
                        avps[:], offset=avps[:].offset,
                        ap=[avps[:].ap[0], [128, 8], [1, 64]]),
                    in1=ap3(rd, hp * 8, [[1, 8], [0, 64]]), op=ALU.mult),
                   reads=[b_avbank[0], b_avbank[1], b_rd], writes=[b_av16[q3]])

        emit_scores(0)
        emit_scores(1)
        emit_av(0)
        emit_scores(2)
        emit_av(1)
        emit_scores(3)
        emit_av(2)
        emit_av(3)

    def stage_att_tail(tau):
        r = tau % 2
        q3 = tau % 3
        xi = tau % NXB
        transpose8(av16[q3], b_av16[q3], avfm[r], b_avfm[r])
        for ns in range(2):
            for ccc in range(NCH):
                t_(lambda e, ns=ns, ccc=ccc, r=r: e.matmul(
                    prj[:, ns * 512:(ns + 1) * 512],
                    avfm[r][:, ccc * 128:(ccc + 1) * 128],
                    wp[:, ccc * C + ns * 512: ccc * C + (ns + 1) * 512],
                    start=(ccc == 0), stop=(ccc == NCH - 1)),
                   reads=[b_avfm[r], b_wp], writes=[b_prjb[ns]],
                   inc=1 if ccc == NCH - 1 else 0)
            v_(lambda e, ns=ns, xi=xi: e.tensor_tensor(
                out=xb[xi][:, ns * 512:(ns + 1) * 512],
                in0=prj[:, ns * 512:(ns + 1) * 512],
                in1=xb[xi][:, ns * 512:(ns + 1) * 512], op=ALU.add),
               reads=[b_prjb[ns], b_xb[xi]], writes=[b_xb[xi]])

    def ln1_full(t2):
        r2 = t2 % 2
        ln_stats(xb[t2 % NXB], b_xb[t2 % NXB], sdum, b_sdum, r2)
        ln_apply(xb[t2 % NXB], b_xb[t2 % NXB], lntm[t2 % 4], b_lntm[t2 % 4], r2)

    N = n_ticks

    def guard(t):
        return 0 <= t < N

    x_load(0)
    ln1_full(0)
    stage_ln1t_pe(0)
    for it in range(N + 8):
        if guard(it):
            stage_ln1t_evac(it)
        if guard(it - 1):
            stage_qkv(it - 1)
        if guard(it - 7):
            stage_ffn_out(it - 7)
        if guard(it + 1):
            x_load(it + 1)
        if guard(it - 4):
            stage_att(it - 4)
        if guard(it - 3):
            stage_pre_b(it - 3)
        if guard(it - 5):
            stage_att_tail(it - 5)
        if guard(it - 6):
            stage_ln2(it - 6)
        if guard(it + 1):
            ln1_full(it + 1)
        if guard(it - 2):
            stage_norm_rope(it - 2)
        if guard(it + 1):
            stage_ln1t_pe(it + 1)

    for i in range(10):
        if cx.count[f"qo{i}"]:
            cx.eng["s"].wait_ge(cx.sems[f"qo{i}"], cx.count[f"qo{i}"])
    es.close()
    return nc


_ROPE_CACHE = {}


def _rope_tables(TT):
    if TT in _ROPE_CACHE:
        return _ROPE_CACHE[TT]
    import jax
    import jax.numpy as jnp
    cpu = jax.devices("cpu")[0]
    with jax.default_device(cpu):
        theta = 1.0 / 10000.0 ** (jnp.arange(0, HD, 2, dtype=jnp.float32) / HD)
        pos = jnp.arange(TT, dtype=jnp.float32)
        freqs = jnp.outer(pos, theta)
        cos = jnp.concatenate([jnp.cos(freqs), jnp.cos(freqs)], axis=-1)
        sin = jnp.concatenate([jnp.sin(freqs), jnp.sin(freqs)], axis=-1)
        cos = np.asarray(jax.device_get(cos)).astype(np.float32)
        sin = np.asarray(jax.device_get(sin)).astype(np.float32)
    sinp = sin.copy()
    sinp[:, :32] = -sinp[:, :32]
    _ROPE_CACHE[TT] = (cos, sinp)
    return cos, sinp


def make_in_maps(L0_attn_x, L0_attn_Wq, L0_attn_Wkv, L0_attn_Wproj,
                 L0_ffn_w_freq_real, L0_ffn_w_freq_imag, L0_ffn_bias,
                 n_cores=N_CORES, n_seq=N_SEQ, n_tb=N_TB):
    TT = n_tb * 128
    cos, sinp = _rope_tables(TT)
    f = np.float32
    wqt = np.ascontiguousarray(np.asarray(L0_attn_Wq, dtype=f).T)
    wkvt = np.ascontiguousarray(np.asarray(L0_attn_Wkv, dtype=f).T)
    wpt = np.ascontiguousarray(np.asarray(L0_attn_Wproj, dtype=f).T)
    wr = np.ascontiguousarray(np.asarray(L0_ffn_w_freq_real, dtype=f))
    wi = np.ascontiguousarray(np.asarray(L0_ffn_w_freq_imag, dtype=f))
    bias = np.ascontiguousarray(np.asarray(L0_ffn_bias, dtype=f).reshape(1, C))
    x = np.asarray(L0_attn_x, dtype=f)
    in_maps = []
    for c in range(n_cores):
        xs = np.ascontiguousarray(x[c * n_seq:(c + 1) * n_seq, :TT, :])
        in_maps.append({
            "x": xs, "wqt": wqt, "wkvt": wkvt, "wpt": wpt,
            "wr": wr, "wi": wi, "bias": bias, "cos": cos, "sinp": sinp,
        })
    return in_maps


def kernel(L0_attn_x, L0_attn_Wq, L0_attn_Wkv, L0_attn_Wproj,
           L0_ffn_w_freq_real, L0_ffn_w_freq_imag, L0_ffn_bias):
    in_maps = make_in_maps(L0_attn_x, L0_attn_Wq, L0_attn_Wkv, L0_attn_Wproj,
                           L0_ffn_w_freq_real, L0_ffn_w_freq_imag, L0_ffn_bias)
    nc = build_program()
    res = run_bass_kernel_spmd(nc, in_maps, core_ids=list(range(N_CORES)))
    out = np.concatenate([r["out"].reshape(N_SEQ, T, C) for r in res.results],
                         axis=0)
    return np.ascontiguousarray(out.astype(np.float32))


# version 14 (final)
# speedup vs baseline: 1.0464x; 1.0010x over previous
import dataclasses
import math
from contextlib import ExitStack

import numpy as np

import concourse.bass as bass
import concourse.mybir as mybir
from concourse.bass_utils import run_bass_kernel_spmd

F32 = mybir.dt.float32
F16 = mybir.dt.float16
I32 = mybir.dt.int32
AF = mybir.ActivationFunctionType
ALU = mybir.AluOpType

N_CORES = 8
B, T, C = 16, 2048, 1024
H, KV, HD = 16, 4, 64
N_SEQ = B // N_CORES
N_TB = T // 128
NCH = C // 128
EPS_LN = 1e-5
EPS_QK = 1e-6
SC = 1.0 / math.sqrt(HD)
SHI = np.float32(2 * np.pi / 1024.0)
def _slo():
    d = np.float64(np.pi) / 512 - np.float64(SHI)
    lo = np.float32(d)
    while 512.0 * (np.float64(SHI) + np.float64(lo)) > np.float64(np.pi):
        lo = np.nextafter(lo, np.float32(-np.inf))
    return lo
SLO = _slo()


def ap3(t, off, dims):
    base = t[:]
    return dataclasses.replace(base, offset=base.offset + off,
                               ap=[base.ap[0]] + [list(d) for d in dims])


def apP(t, p0, np_, off, dims):
    base = t[p0:p0 + np_]
    return dataclasses.replace(base, offset=base.offset + off,
                               ap=[base.ap[0]] + [list(d) for d in dims])


class Buf:
    __slots__ = ("name", "writer", "readers")

    def __init__(self, name):
        self.name = name
        self.writer = {}
        self.readers = {}


class Ctx:
    def __init__(self, nc, es):
        self.nc = nc
        self.es = es
        self.eng = {"t": nc.tensor, "v": nc.vector, "s": nc.scalar,
                    "g": nc.gpsimd, "y": nc.sync}
        self.sems = {}
        self.count = {}
        self.waited = {k: {} for k in ("t", "v", "s", "g", "y")}
        qnames = ["qw"] + [f"qx{i}" for i in range(12)] + [f"qo{i}" for i in range(12)]
        for name in ["t", "v", "s", "g"] + qnames:
            self.sems[name] = es.enter_context(nc.semaphore("sem_" + name))
            self.count[name] = 0
        self.qeng = {"qw": "g"}
        for i in range(12):
            self.qeng[f"qx{i}"] = "y"
            self.qeng[f"qo{i}"] = "s"
        self.fencetok = {}

    def fence(self):
        self.fencetok = {s: v for s, v in self.count.items() if v > 0}

    def _waits(self, ename, need):
        w = self.waited[ename]
        for s in sorted(need):
            v = need[s]
            if s == ename and v > self.count[s]:
                continue
            if v > w.get(s, 0):
                self.eng[ename].wait_ge(self.sems[s], v)
                w[s] = v

    def _collect(self, reads, writes):
        need = dict(self.fencetok)
        for b in reads:
            for s, v in b.writer.items():
                if v > need.get(s, 0):
                    need[s] = v
        for b in writes:
            for d in (b.writer, b.readers):
                for s, v in d.items():
                    if v > need.get(s, 0):
                        need[s] = v
        return need

    def op(self, ename, fn, reads=(), writes=(), inc=1):
        self._waits(ename, self._collect(reads, writes))
        inst = fn(self.eng[ename])
        if inc:
            self.count[ename] += 1
            inst.then_inc(self.sems[ename], 1)
        tok = self.count[ename] + (0 if inc else 1)
        for b in writes:
            b.writer = {ename: tok}
            b.readers = {}
        for b in reads:
            if b.readers.get(ename, 0) < tok:
                b.readers[ename] = tok
        return inst

    def dma(self, q, fn, reads=(), writes=()):
        ename = self.qeng[q]
        need = self._collect(reads, writes)
        if self.count[q] > need.get(q, 0):
            need[q] = self.count[q]
        self._waits(ename, need)
        inst = fn(self.eng[ename])
        self.count[q] += 16
        inst.then_inc(self.sems[q], 16)
        tok = self.count[q]
        for b in writes:
            b.writer = {q: tok}
            b.readers = {}
        for b in reads:
            if b.readers.get(q, 0) < tok:
                b.readers[q] = tok

    def sb(self, name, shape, dtype):
        t = self.es.enter_context(self.nc.sbuf_tensor(name, shape, dtype))
        return t, Buf(name)

    def ps(self, name, shape, dtype=F32):
        t = self.es.enter_context(self.nc.psum_tensor(name, shape, dtype))
        return t, Buf(name)


def build_program(n_seq=N_SEQ, n_tb=N_TB):
    TT = n_tb * 128
    nc = bass.Bass("TRN2", target_bir_lowering=False, debug=False)

    d_x = nc.dram_tensor("x", [n_seq, TT, C], F32, kind="ExternalInput")
    d_wqt = nc.dram_tensor("wqt", [C, C], F32, kind="ExternalInput")
    d_wkvt = nc.dram_tensor("wkvt", [C, 512], F32, kind="ExternalInput")
    d_wpt = nc.dram_tensor("wpt", [C, C], F32, kind="ExternalInput")
    d_wr = nc.dram_tensor("wr", [C, C], F32, kind="ExternalInput")
    d_wi = nc.dram_tensor("wi", [C, C], F32, kind="ExternalInput")
    d_bias = nc.dram_tensor("bias", [1, C], F32, kind="ExternalInput")
    d_cos = nc.dram_tensor("cos", [TT, HD], F32, kind="ExternalInput")
    d_sinp = nc.dram_tensor("sinp", [TT, HD], F32, kind="ExternalInput")
    d_out = nc.dram_tensor("out", [n_seq, TT, C], F32, kind="ExternalOutput")

    es = ExitStack()
    cx = Ctx(nc, es)

    wq, b_wq = cx.sb("wq16", [128, NCH * C], F16)
    wkv, b_wkv = cx.sb("wkv16", [128, NCH * 512], F16)
    wp, b_wp = cx.sb("wp16", [128, NCH * C], F16)
    mx, b_mx = cx.sb("mx16", [128, NCH * C], F16)
    cost, b_cos = cx.sb("cos16", [128, n_tb * HD], F16)
    sint, b_sin = cx.sb("sinp16", [128, n_tb * HD], F16)
    kt = [None, None]
    b_ktslots = [[Buf(f"kts{r}_{i}") for i in range(n_tb)] for r in range(2)]
    for reg in range(2):
        kt[reg] = es.enter_context(nc.sbuf_tensor(f"ktil{reg}", [128, TT], F16))
    vpl, _ = cx.sb("vplus", [128, n_tb * 260], F16)
    b_vpslots = [Buf(f"vps{i}") for i in range(n_tb)]
    bias16, b_bias = cx.sb("bias16", [1, C], F16)
    ones1, b_ones1 = cx.sb("ones1", [1, 128], F16)
    ident, b_ident = cx.sb("ident", [128, 128], F16)
    m0, b_m0 = cx.sb("mask0", [128, 128], F16)
    m4, b_m4 = cx.sb("mask4", [128, 128], F16)

    sring, _ = cx.ps("sring", [128, 3 * 512])
    b_slot = [Buf(f"slot{i}") for i in range(3)]
    avps, _ = cx.ps("avps", [128, 1024])
    b_avbank = [Buf("avb0"), Buf("avb1")]
    ptps, b_pt = cx.ps("ptps", [128, 1024], F16)
    prj, _ = cx.ps("prj", [128, 1024])
    b_prjb = [Buf("prj0"), Buf("prj1")]

    def t_(fn, reads=(), writes=(), inc=1):
        return cx.op("t", fn, reads, writes, inc)

    def v_(fn, reads=(), writes=(), inc=1):
        return cx.op("v", fn, reads, writes, inc)

    def s_(fn, reads=(), writes=(), inc=1):
        return cx.op("s", fn, reads, writes, inc)

    def g_(fn, reads=(), writes=(), inc=1):
        return cx.op("g", fn, reads, writes, inc)

    pro_es = ExitStack()
    kio, b_kio = (pro_es.enter_context(nc.sbuf_tensor("kio", [128, C], F32)), Buf("kio"))
    jcol, b_jcol = (pro_es.enter_context(nc.sbuf_tensor("jcol", [128, 1], F32)), Buf("jcol"))
    g_(lambda e: e.iota(kio[:], pattern=[[1, C]], base=0, channel_multiplier=0,
                        allow_small_or_imprecise_dtypes=True), writes=[b_kio])
    g_(lambda e: e.iota(jcol[:], pattern=[[0, 1]], base=0, channel_multiplier=1,
                        allow_small_or_imprecise_dtypes=True), writes=[b_jcol])
    v_(lambda e: e.tensor_scalar(out=ident[:], in0=kio[:, 0:128], scalar1=jcol[:],
                                 scalar2=None, op0=ALU.is_equal),
       reads=[b_kio, b_jcol], writes=[b_ident])
    v_(lambda e: e.tensor_scalar(out=m0[:], in0=kio[:, 0:128], scalar1=jcol[:],
                                 scalar2=None, op0=ALU.is_ge),
       reads=[b_kio, b_jcol], writes=[b_m0])
    v_(lambda e: e.tensor_scalar(out=m4[:], in0=kio[:, 0:128], scalar1=jcol[:],
                                 scalar2=None, op0=ALU.is_lt),
       reads=[b_kio, b_jcol], writes=[b_m4])
    v_(lambda e: e.memset(ones1[:], 1.0), writes=[b_ones1])

    cx.dma("qw", lambda e: e.dma_start(
        out=ap3(wq, 0, [[C, NCH], [1, C]]),
        in_=d_wqt.ap().rearrange("(cc p) i -> p cc i", p=128)), writes=[b_wq])
    cx.dma("qw", lambda e: e.dma_start(
        out=ap3(wkv, 0, [[512, NCH], [1, 512]]),
        in_=d_wkvt.ap().rearrange("(cc p) j -> p cc j", p=128)), writes=[b_wkv])
    cx.dma("qw", lambda e: e.dma_start(
        out=ap3(wp, 0, [[C, NCH], [1, C]]),
        in_=d_wpt.ap().rearrange("(ic p) o -> p ic o", p=128)), writes=[b_wp])
    cx.dma("qw", lambda e: e.dma_start(
        out=ap3(cost, 0, [[HD, n_tb], [1, HD]]),
        in_=d_cos.ap().rearrange("(tb p) d -> p tb d", p=128)), writes=[b_cos])
    cx.dma("qw", lambda e: e.dma_start(
        out=ap3(sint, 0, [[HD, n_tb], [1, HD]]),
        in_=d_sinp.ap().rearrange("(tb p) d -> p tb d", p=128)), writes=[b_sin])
    cx.dma("qw", lambda e: e.dma_start(out=bias16[:], in_=d_bias.ap()),
           writes=[b_bias])

    with ExitStack() as mes:
        def msb(nm, sh, dt):
            return mes.enter_context(nc.sbuf_tensor(nm, sh, dt)), Buf(nm)

        c16, b_c16 = msb("c16", [128, NCH * C], F16)
        s16, b_s16 = msb("s16", [128, NCH * C], F16)
        wr16, b_wr16 = msb("wr16", [128, NCH * C], F16)
        wi16, b_wi16 = msb("wi16", [128, NCH * C], F16)
        pt16, b_pt16 = msb("ptm16", [128, NCH * C], F16)
        qt16, b_qt16 = msb("qtm16", [128, NCH * C], F16)
        jk, b_jk = msb("jk", [128, C], F32)
        red, b_red = msb("red", [128, C], F32)
        scr, b_scr = msb("mscr", [128, C], F32)
        qi, b_qi = msb("qi", [128, C], I32)
        ang, b_ang = msb("angm", [128, C], F32)

        cx.dma("qw", lambda e: e.dma_start(
            out=ap3(wr16, 0, [[C, NCH], [1, C]]),
            in_=d_wr.ap().rearrange("(kc p) o -> p kc o", p=128)), writes=[b_wr16])
        cx.dma("qw", lambda e: e.dma_start(
            out=ap3(wi16, 0, [[C, NCH], [1, C]]),
            in_=d_wi.ap().rearrange("(kc p) o -> p kc o", p=128)), writes=[b_wi16])

        for cc in range(NCH):
            v_(lambda e, cc=cc: e.tensor_scalar(
                out=scr[:, 0:1], in0=jcol[:], scalar1=float(128 * cc),
                scalar2=None, op0=ALU.add), reads=[b_jcol], writes=[b_scr])
            v_(lambda e: e.tensor_scalar(
                out=jk[:], in0=kio[:], scalar1=scr[:, 0:1], scalar2=None,
                op0=ALU.mult), reads=[b_kio, b_scr], writes=[b_jk])
            v_(lambda e: e.tensor_scalar(
                out=qi[:], in0=jk[:], scalar1=float(2.0 ** -10),
                scalar2=float(0.5 + 2.0 ** -12), op0=ALU.mult, op1=ALU.add),
               reads=[b_jk], writes=[b_qi])
            v_(lambda e: e.tensor_copy(out=red[:], in_=qi[:]),
               reads=[b_qi], writes=[b_red])
            v_(lambda e: e.scalar_tensor_tensor(
                out=red[:], in0=red[:], scalar=-1024.0, in1=jk[:],
                op0=ALU.mult, op1=ALU.add), reads=[b_red, b_jk], writes=[b_red])
            v_(lambda e: e.tensor_single_scalar(
                out=scr[:], in_=red[:], scalar=512.0, op=ALU.is_ge),
               reads=[b_red], writes=[b_scr])
            v_(lambda e: e.scalar_tensor_tensor(
                out=red[:], in0=scr[:], scalar=-1024.0, in1=red[:],
                op0=ALU.mult, op1=ALU.add), reads=[b_scr, b_red], writes=[b_red])
            v_(lambda e: e.tensor_single_scalar(
                out=scr[:], in_=red[:], scalar=-512.0, op=ALU.is_lt),
               reads=[b_red], writes=[b_scr])
            v_(lambda e: e.scalar_tensor_tensor(
                out=red[:], in0=scr[:], scalar=1024.0, in1=red[:],
                op0=ALU.mult, op1=ALU.add), reads=[b_scr, b_red], writes=[b_red])
            v_(lambda e: e.tensor_scalar(
                out=ang[:], in0=red[:], scalar1=float(SHI), scalar2=None,
                op0=ALU.mult), reads=[b_red], writes=[b_ang])
            v_(lambda e: e.scalar_tensor_tensor(
                out=ang[:], in0=red[:], scalar=float(SLO), in1=ang[:],
                op0=ALU.mult, op1=ALU.add), reads=[b_red, b_ang], writes=[b_ang])
            s_(lambda e, cc=cc: e.activation(
                out=s16[:, cc * C:(cc + 1) * C], in_=ang[:], func=AF.Sin),
               reads=[b_ang], writes=[b_s16])
            v_(lambda e: e.tensor_single_scalar(
                out=scr[:], in_=red[:], scalar=256.0, op=ALU.is_ge),
               reads=[b_red], writes=[b_scr])
            v_(lambda e: e.scalar_tensor_tensor(
                out=red[:], in0=scr[:], scalar=-1024.0, in1=red[:],
                op0=ALU.mult, op1=ALU.add), reads=[b_scr, b_red], writes=[b_red])
            v_(lambda e: e.tensor_scalar(
                out=red[:], in0=red[:], scalar1=256.0, scalar2=None,
                op0=ALU.add), reads=[b_red], writes=[b_red])
            v_(lambda e: e.tensor_scalar(
                out=ang[:], in0=red[:], scalar1=float(SHI), scalar2=None,
                op0=ALU.mult), reads=[b_red], writes=[b_ang])
            v_(lambda e: e.scalar_tensor_tensor(
                out=ang[:], in0=red[:], scalar=float(SLO), in1=ang[:],
                op0=ALU.mult, op1=ALU.add), reads=[b_red, b_ang], writes=[b_ang])
            s_(lambda e, cc=cc: e.activation(
                out=c16[:, cc * C:(cc + 1) * C], in_=ang[:], func=AF.Sin),
               reads=[b_ang], writes=[b_c16])
        def negate_wi():
            v_(lambda e: e.tensor_scalar(out=wi16[:], in0=wi16[:], scalar1=-1.0,
                                         scalar2=None, op0=ALU.mult),
               reads=[b_wi16], writes=[b_wi16])
        for dst16, b_dst, terms in (
            (pt16, b_pt16, [(wr16, b_wr16, c16, b_c16), (wi16, b_wi16, s16, b_s16)]),
            (qt16, b_qt16, [(wi16, b_wi16, c16, b_c16), (wr16, b_wr16, s16, b_s16)]),
        ):
            if dst16 is qt16:
                negate_wi()
            for oc in range(NCH):
                for ns in range(2):
                    k_i = 0
                    nmm = 2 * NCH
                    for (wsb, b_w, msb_, b_m) in terms:
                        for kc in range(NCH):
                            last = k_i == nmm - 1
                            t_(lambda e, oc=oc, ns=ns, kc=kc, wsb=wsb, msb_=msb_,
                               first=(k_i == 0), last=last: e.matmul(
                                   prj[:, ns * 512:(ns + 1) * 512],
                                   wsb[:, kc * C + oc * 128: kc * C + oc * 128 + 128],
                                   msb_[:, kc * C + ns * 512: kc * C + (ns + 1) * 512],
                                   start=first, stop=last),
                               reads=[b_w, b_m], writes=[b_prjb[ns]],
                               inc=1 if last else 0)
                            k_i += 1
                    s_(lambda e, oc=oc, ns=ns, dst16=dst16: e.activation(
                        out=dst16[:, oc * C + ns * 512: oc * C + (ns + 1) * 512],
                        in_=prj[:, ns * 512:(ns + 1) * 512], func=AF.Copy,
                        scale=float(2.0 ** -10)),
                       reads=[b_prjb[ns]], writes=[b_dst])
        for jc in range(NCH):
            for ns in range(2):
                k_i = 0
                for (lsb, b_l, msb_, b_m) in ((pt16, b_pt16, c16, b_c16),
                                              (qt16, b_qt16, s16, b_s16)):
                    for oc in range(NCH):
                        last = k_i == 2 * NCH - 1
                        t_(lambda e, jc=jc, ns=ns, oc=oc, lsb=lsb, msb_=msb_,
                           first=(k_i == 0), last=last: e.matmul(
                               prj[:, ns * 512:(ns + 1) * 512],
                               lsb[:, oc * C + jc * 128: oc * C + jc * 128 + 128],
                               msb_[:, oc * C + ns * 512: oc * C + (ns + 1) * 512],
                               start=first, stop=last),
                           reads=[b_l, b_m], writes=[b_prjb[ns]],
                           inc=1 if last else 0)
                        k_i += 1
                s_(lambda e, jc=jc, ns=ns: e.activation(
                    out=mx[:, jc * C + ns * 512: jc * C + (ns + 1) * 512],
                    in_=prj[:, ns * 512:(ns + 1) * 512], func=AF.Copy),
                   reads=[b_prjb[ns]], writes=[b_mx])
    pro_es.close()
    cx.fence()

    NXB = 11
    xb, b_xb = zip(*[cx.sb(f"xb{i}", [128, C], F32) for i in range(NXB)])
    lnT, b_lnT = zip(*[cx.sb(f"lnT{i}", [128, C], F16) for i in range(2)])
    ln2T, b_ln2T = zip(*[cx.sb(f"ln2T{i}", [128, C], F16) for i in range(1)])
    lntm, b_lntm = zip(*[cx.sb(f"lntm{i}", [128, C], F16) for i in range(4)])
    ln2tm, b_ln2tm = zip(*[cx.sb(f"ln2tm{i}", [128, C], F16) for i in range(3)])
    qtm, b_qtm = zip(*[cx.sb(f"qtm{i}", [128, C], F16) for i in range(3)])
    qtm2, b_qtm2 = zip(*[cx.sb(f"qtm2_{i}", [128, C], F16) for i in range(3)])
    qfm, b_qfm = zip(*[cx.sb(f"qfm{i}", [128, C], F16) for i in range(3)])
    ktm, b_ktm = zip(*[cx.sb(f"ktm{i}", [128, 256], F16) for i in range(3)])
    ktm2, b_ktm2 = zip(*[cx.sb(f"ktm2_{i}", [128, 256], F16) for i in range(3)])
    sscr, b_sscr = zip(*[cx.sb(f"sscr{i}", [128, C], F32) for i in range(2)])
    sdum, b_sdum = cx.sb("sdum", [128, C], F32)
    exs, b_exs = zip(*[cx.sb(f"exs{i}", [128, 2 * 2560], F16) for i in range(2)])
    b_exsg = [[Buf(f"exsg{i}_{g}") for g in range(2)] for i in range(2)]
    av16, b_av16 = zip(*[cx.sb(f"av16_{i}", [128, C], F16) for i in range(3)])
    avfm, b_avfm = zip(*[cx.sb(f"avfm{i}", [128, C], F16) for i in range(1)])
    st_, b_st = zip(*[cx.sb(f"stats{i}", [128, 16], F32) for i in range(8)])
    qss, b_qss = zip(*[cx.sb(f"qss{i}", [128, 24], F32) for i in range(2)])
    rcq, b_rcq = zip(*[cx.sb(f"rcq{i}", [128, 24], F32) for i in range(2)])
    rdn, b_rdn = zip(*[cx.sb(f"rdn{i}", [128, 16], F32) for i in range(2)])


    n_ticks = n_seq * n_tb

    def x_load(tau):
        s, i = divmod(tau, n_tb)
        xi = tau % NXB
        cx.dma(f"qx{xi}", lambda e: e.dma_start(
            out=xb[xi][:], in_=d_x.ap()[s, i * 128:(i + 1) * 128, :]),
            writes=[b_xb[xi]])


    def ln_stats(xbuf, b_xbuf, scrt, b_scrt, sti):
        stt, b_stt = st_[sti], b_st[sti]
        s_(lambda e: e.activation(out=scrt[:], in_=xbuf[:], func=AF.Square,
                                  accum_out=stt[:, 0:1]),
           reads=[b_xbuf], writes=[b_scrt, b_stt])
        s_(lambda e: e.activation(out=scrt[:], in_=xbuf[:], func=AF.Copy,
                                  accum_out=stt[:, 1:2]),
           reads=[b_xbuf], writes=[b_scrt, b_stt])
        s_(lambda e: e.activation(out=stt[:, 2:3], in_=stt[:, 1:2], func=AF.Copy,
                                  scale=float(2.0 ** -10)),
           reads=[b_stt], writes=[b_stt])
        s_(lambda e: e.activation(out=stt[:, 3:4], in_=stt[:, 2:3],
                                  func=AF.Square),
           reads=[b_stt], writes=[b_stt])
        s_(lambda e: e.activation(out=stt[:, 4:5], in_=stt[:, 3:4], func=AF.Copy,
                                  scale=-1.0, bias=float(EPS_LN)),
           reads=[b_stt], writes=[b_stt])
        s_(lambda e: e.activation(out=stt[:, 6:7], in_=stt[:, 0:1], func=AF.Ln,
                                  scale=float(2.0 ** -10), bias=stt[:, 4:5]),
           reads=[b_stt], writes=[b_stt])
        s_(lambda e: e.activation(out=stt[:, 7:8], in_=stt[:, 6:7], func=AF.Exp,
                                  scale=-0.5), reads=[b_stt], writes=[b_stt])
        s_(lambda e: e.activation(out=stt[:, 9:10], in_=stt[:, 2:3], func=AF.Copy,
                                  scale=stt[:, 7:8]),
           reads=[b_stt], writes=[b_stt])

    def ln_apply(xbuf, b_xbuf, dst, b_dst, sti):
        stt, b_stt = st_[sti], b_st[sti]
        v_(lambda e: e.tensor_scalar(out=dst[:], in0=xbuf[:], scalar1=stt[:, 7:8],
                                     scalar2=stt[:, 9:10], op0=ALU.mult,
                                     op1=ALU.subtract),
           reads=[b_xbuf, b_stt], writes=[b_dst])

    def transpose8(srct, b_src, dstt, b_dst, nch=NCH):
        for ccc in range(nch):
            t_(lambda e, ccc=ccc: e.transpose(
                ptps[:, ccc * 128:(ccc + 1) * 128],
                srct[:, ccc * 128:(ccc + 1) * 128], ident[:]),
               reads=[b_src, b_ident], writes=[b_pt],
               inc=1 if ccc == nch - 1 else 0)
        v_(lambda e: e.tensor_copy(out=dstt[:, 0:nch * 128],
                                   in_=ptps[:, 0:nch * 128]),
           reads=[b_pt], writes=[b_dst])

    def stage_ln2(ptau):
        pr = ptau % 2
        q3 = ptau % 3
        pxi = ptau % NXB
        ln_stats(xb[pxi], b_xb[pxi], sdum, b_sdum, 4 + pr)
        ln_apply(xb[pxi], b_xb[pxi], ln2tm[q3], b_ln2tm[q3], 4 + pr)

    def stage_ffn_out(ptau):
        pr = ptau % 2
        q3 = ptau % 3
        ps_, pi = divmod(ptau, n_tb)
        pxi = ptau % NXB
        transpose8(ln2tm[q3], b_ln2tm[q3], ln2T[0], b_ln2T[0])
        for ns in range(2):
            for ccc in range(NCH):
                t_(lambda e, ccc=ccc, ns=ns, pr=pr: e.matmul(
                    prj[:, ns * 512:(ns + 1) * 512],
                    ln2T[0][:, ccc * 128:(ccc + 1) * 128],
                    mx[:, ccc * C + ns * 512: ccc * C + (ns + 1) * 512],
                    start=(ccc == 0), stop=False),
                   reads=[b_ln2T[0], b_mx], writes=[b_prjb[ns]], inc=0)
            t_(lambda e, ns=ns: e.matmul(
                prj[:, ns * 512:(ns + 1) * 512], ones1[:],
                bias16[:, ns * 512:(ns + 1) * 512], start=False, stop=True),
               reads=[b_ones1, b_bias], writes=[b_prjb[ns]], inc=1)
            v_(lambda e, ns=ns, pxi=pxi: e.tensor_tensor(
                out=xb[pxi][:, ns * 512:(ns + 1) * 512],
                in0=prj[:, ns * 512:(ns + 1) * 512],
                in1=xb[pxi][:, ns * 512:(ns + 1) * 512], op=ALU.add),
               reads=[b_prjb[ns], b_xb[pxi]], writes=[b_xb[pxi]])
        cx.dma(f"qo{pxi}", lambda e, ps_=ps_, pi=pi, pxi=pxi: e.dma_start(
            out=d_out.ap()[ps_, pi * 128:(pi + 1) * 128, :], in_=xb[pxi][:]),
            reads=[b_xb[pxi]])

    slot_ctr = [0]

    def stage_ln1t_pe(tau):
        for ccc in range(NCH):
            t_(lambda e, ccc=ccc, tau=tau: e.transpose(
                ptps[:, ccc * 128:(ccc + 1) * 128],
                lntm[tau % 4][:, ccc * 128:(ccc + 1) * 128], ident[:]),
               reads=[b_lntm[tau % 4], b_ident], writes=[b_pt],
               inc=1 if ccc == NCH - 1 else 0)

    def stage_ln1t_evac(tau):
        r = tau % 2
        v_(lambda e, r=r: e.tensor_copy(out=lnT[r][:, 0:C], in_=ptps[:, 0:C]),
           reads=[b_pt], writes=[b_lnT[r]])

    def stage_qkv(tau):
        s, i = divmod(tau, n_tb)
        r = tau % 2
        q3 = tau % 3
        xi = tau % NXB
        for ns in range(2):
            for ccc in range(NCH):
                t_(lambda e, ns=ns, ccc=ccc, r=r: e.matmul(
                    prj[:, ns * 512:(ns + 1) * 512],
                    lnT[r][:, ccc * 128:(ccc + 1) * 128],
                    wq[:, ccc * C + ns * 512: ccc * C + (ns + 1) * 512],
                    start=(ccc == 0), stop=(ccc == NCH - 1)),
                   reads=[b_lnT[r], b_wq], writes=[b_prjb[ns]],
                   inc=1 if ccc == NCH - 1 else 0)
            v_(lambda e, ns=ns, r=r: e.tensor_copy(
                out=qtm[q3][:, ns * 512:(ns + 1) * 512],
                in_=prj[:, ns * 512:(ns + 1) * 512]),
               reads=[b_prjb[ns]], writes=[b_qtm[q3]])
        for ccc in range(NCH):
            t_(lambda e, ccc=ccc, r=r: e.matmul(
                prj[:, 0:512], lnT[r][:, ccc * 128:(ccc + 1) * 128],
                wkv[:, ccc * 512:(ccc + 1) * 512],
                start=(ccc == 0), stop=(ccc == NCH - 1)),
               reads=[b_lnT[r], b_wkv], writes=[b_prjb[0]],
               inc=1 if ccc == NCH - 1 else 0)
        v_(lambda e, r=r: e.tensor_copy(out=ktm[q3][:, 0:256], in_=prj[:, 0:256]),
           reads=[b_prjb[0]], writes=[b_ktm[q3]])
        v_(lambda e, i=i: e.tensor_copy(
            out=ap3(vpl, i * 260, [[65, 4], [1, 64]]),
            in_=dataclasses.replace(prj[:], offset=prj[:].offset + 256,
                                    ap=[prj[:].ap[0], [64, 4], [1, 64]])),
           reads=[b_prjb[0]], writes=[b_vpslots[i]])
        v_(lambda e, i=i: e.memset(ap3(vpl, i * 260 + 64, [[65, 4], [1, 1]]), 1.0),
           writes=[b_vpslots[i]])


    def stage_norm_rope(tau):
        s, i = divmod(tau, n_tb)
        r = tau % 2
        q3 = tau % 3
        qs, b_qs = qss[r], b_qss[r]
        rc, b_rc = rcq[r], b_rcq[r]
        sc2, b_sc2 = sscr[r], b_sscr[r]
        s_(lambda e, r=r: e.activation(out=sc2[:], in_=qtm[q3][:], func=AF.Square),
           reads=[b_qtm[q3]], writes=[b_sc2])
        v_(lambda e, r=r: e.tensor_reduce(
            out=ap3(qss[r], 0, [[1, 16], [1, 1]]),
            in_=ap3(sscr[r], 0, [[64, 16], [1, 64]]),
            axis=mybir.AxisListType.X, op=ALU.add),
           reads=[b_sc2], writes=[b_qs])
        s_(lambda e, r=r: e.activation(out=sc2[:, 0:256], in_=ktm[q3][:, 0:256],
                                       func=AF.Square),
           reads=[b_ktm[q3]], writes=[b_sc2])
        v_(lambda e, r=r: e.tensor_reduce(
            out=ap3(qss[r], 16, [[1, 4], [1, 1]]),
            in_=ap3(sscr[r], 0, [[64, 4], [1, 64]]),
            axis=mybir.AxisListType.X, op=ALU.add),
           reads=[b_sc2], writes=[b_qs])
        s_(lambda e, r=r: e.activation(out=ap3(qss[r], 0, [[1, 20]]),
                                       in_=ap3(qss[r], 0, [[1, 20]]),
                                       func=AF.Sqrt),
           reads=[b_qs], writes=[b_qs])
        v_(lambda e, r=r: e.tensor_scalar(out=ap3(qss[r], 0, [[1, 20]]),
                                          in0=ap3(qss[r], 0, [[1, 20]]),
                                          scalar1=float(EPS_QK), scalar2=None,
                                          op0=ALU.add), reads=[b_qs], writes=[b_qs])
        v_(lambda e, r=r: e.reciprocal(out=ap3(rcq[r], 0, [[1, 20]]),
                                       in_=ap3(qss[r], 0, [[1, 20]])),
           reads=[b_qs], writes=[b_rc])
        v_(lambda e, r=r: e.tensor_scalar(out=ap3(rcq[r], 0, [[1, 16]]),
                                          in0=ap3(rcq[r], 0, [[1, 16]]),
                                          scalar1=float(SC), scalar2=None,
                                          op0=ALU.mult), reads=[b_rc], writes=[b_rc])
        for (srcb, b_srcb, dstb, b_dstb, nh, roff) in (
                (qtm[q3], b_qtm[q3], qtm2[q3], b_qtm2[q3], 16, 0),
                (ktm[q3], b_ktm[q3], ktm2[q3], b_ktm2[q3], 4, 16)):
            v_(lambda e, srcb=srcb, dstb=dstb, nh=nh, roff=roff: e.tensor_tensor(
                out=ap3(dstb, 0, [[64, nh], [1, 64]]),
                in0=ap3(srcb, 0, [[64, nh], [1, 64]]),
                in1=ap3(rc, roff, [[1, nh], [0, 64]]), op=ALU.mult),
               reads=[b_srcb, b_rc], writes=[b_dstb])
            v_(lambda e, srcb=srcb, dstb=dstb, nh=nh, i=i: e.tensor_tensor(
                out=ap3(srcb, 0, [[64, nh], [32, 2], [1, 32]]),
                in0=dataclasses.replace(
                    dstb[:], offset=dstb[:].offset + 32,
                    ap=[dstb[:].ap[0], [64, nh], [-32, 2], [1, 32]]),
                in1=ap3(sint, i * HD, [[0, nh], [32, 2], [1, 32]]),
                op=ALU.mult),
               reads=[b_dstb, b_sin], writes=[b_srcb])
            v_(lambda e, dstb=dstb, nh=nh, i=i: e.tensor_tensor(
                out=ap3(dstb, 0, [[64, nh], [1, 64]]),
                in0=ap3(dstb, 0, [[64, nh], [1, 64]]),
                in1=ap3(cost, i * HD, [[0, nh], [1, 64]]), op=ALU.mult),
               reads=[b_dstb, b_cos], writes=[b_dstb])
            v_(lambda e, srcb=srcb, dstb=dstb, nh=nh: e.tensor_tensor(
                out=ap3(dstb, 0, [[64, nh], [1, 64]]),
                in0=ap3(dstb, 0, [[64, nh], [1, 64]]),
                in1=ap3(srcb, 0, [[64, nh], [1, 64]]), op=ALU.add),
               reads=[b_dstb, b_srcb], writes=[b_dstb])


    def stage_pre_b(tau):
        s, i = divmod(tau, n_tb)
        r = tau % 2
        q3 = tau % 3
        for h in range(H):
            rowb = ((h // 4) % 2) * 64
            strip = (h % 4) + 4 * (h // 8)
            t_(lambda e, h=h, rowb=rowb, strip=strip, r=r: e.transpose(
                dataclasses.replace(ptps[rowb:rowb + 64],
                                    offset=ptps[rowb:rowb + 64].offset + strip * 128,
                                    ap=[ptps[rowb:rowb + 64].ap[0], [1, 128]]),
                qtm2[q3][:, h * 64:(h + 1) * 64], ident[:]),
               reads=[b_qtm2[q3], b_ident], writes=[b_pt],
               inc=1 if h == H - 1 else 0)
        v_(lambda e, r=r: e.tensor_copy(out=qfm[q3][:, 0:C], in_=ptps[:, 0:C]),
           reads=[b_pt], writes=[b_qfm[q3]])
        for reg in range(2):
            t_(lambda e, reg=reg, r=r: e.transpose(
                ptps[:, reg * 128:(reg + 1) * 128],
                ktm2[q3][:, reg * 128:(reg + 1) * 128], ident[:]),
               reads=[b_ktm2[q3], b_ident], writes=[b_pt],
               inc=1 if reg == 1 else 0)
        for reg in range(2):
            v_(lambda e, reg=reg, i=i: e.tensor_copy(
                out=kt[reg][:, i * 128:(i + 1) * 128],
                in_=ptps[:, reg * 128:(reg + 1) * 128]),
               reads=[b_pt], writes=[b_ktslots[reg][i]])

    def stage_att(tau):
        s, i = divmod(tau, n_tb)
        r = tau % 2
        q3 = tau % 3
        xi = tau % NXB
        kcs = list(range(max(0, i - 4), i + 1))
        nkc = len(kcs)
        exsb, b_exsb = exs[r], b_exs[r]

        def emit_scores(g):
            reg, row = g // 2, (g % 2) * 64
            pend = []
            for kidx, kc in enumerate(kcs):
                sl = slot_ctr[0] % 3
                slot_ctr[0] += 1
                for j in range(4):
                    h = 4 * g + j
                    strip = (h % 4) + 4 * (h // 8)
                    t_(lambda e, sl=sl, j=j, reg=reg, row=row, kc=kc,
                       strip=strip, r=r, g=g: e.matmul(
                           sring[:, sl * 512 + j * 128: sl * 512 + (j + 1) * 128],
                           apP(kt[reg], row, 64, kc * 128, [[1, 128]]),
                           apP(qfm[q3], row, 64, strip * 128, [[1, 128]]),
                           start=(j == 0), stop=(j == 3)),
                       reads=[b_ktslots[reg][kc], b_qfm[q3]], writes=[b_slot[sl]],
                       inc=1 if j == 3 else 0)
                pend.append((kidx, sl))
                while len(pend) >= 2:
                    if pend[1][1] == pend[0][1] + 1:
                        k0, s0 = pend[0]
                        del pend[0:2]
                        s_(lambda e, k0=k0, s0=s0, g=g: e.activation(
                            out=exsb[:, (g % 2) * 2560 + k0 * 512:
                                     (g % 2) * 2560 + (k0 + 2) * 512],
                            in_=sring[:, s0 * 512:(s0 + 2) * 512], func=AF.Exp),
                           reads=[b_slot[s0], b_slot[s0 + 1]],
                           writes=[b_exsg[r][g % 2]])
                    else:
                        k0, s0 = pend.pop(0)
                        s_(lambda e, k0=k0, s0=s0, g=g: e.activation(
                            out=exsb[:, (g % 2) * 2560 + k0 * 512:
                                     (g % 2) * 2560 + (k0 + 1) * 512],
                            in_=sring[:, s0 * 512:(s0 + 1) * 512], func=AF.Exp),
                           reads=[b_slot[s0]], writes=[b_exsg[r][g % 2]])
            for k0, s0 in pend:
                s_(lambda e, k0=k0, s0=s0, g=g: e.activation(
                    out=exsb[:, (g % 2) * 2560 + k0 * 512:
                             (g % 2) * 2560 + (k0 + 1) * 512],
                    in_=sring[:, s0 * 512:(s0 + 1) * 512], func=AF.Exp),
                   reads=[b_slot[s0]], writes=[b_exsg[r][g % 2]])
            pend.clear()
            v_(lambda e, kidx=nkc - 1, g=g: e.tensor_tensor(
                out=ap3(exsb, (g % 2) * 2560 + kidx * 512, [[128, 4], [1, 128]]),
                in0=ap3(exsb, (g % 2) * 2560 + kidx * 512, [[128, 4], [1, 128]]),
                in1=ap3(m0, 0, [[0, 4], [1, 128]]), op=ALU.mult),
               reads=[b_exsg[r][g % 2], b_m0], writes=[b_exsg[r][g % 2]])
            if i >= 4:
                v_(lambda e, g=g: e.tensor_tensor(
                    out=ap3(exsb, (g % 2) * 2560, [[128, 4], [1, 128]]),
                    in0=ap3(exsb, (g % 2) * 2560, [[128, 4], [1, 128]]),
                    in1=ap3(m4, 0, [[0, 4], [1, 128]]), op=ALU.mult),
                   reads=[b_exsg[r][g % 2], b_m4], writes=[b_exsg[r][g % 2]])

        def emit_av(g):
            bank = g % 2
            for kidx, kc in enumerate(kcs):
                for j in range(4):
                    first = kidx == 0 and j == 0
                    last = kidx == nkc - 1 and j == 3
                    t_(lambda e, kidx=kidx, kc=kc, j=j, g=g, bank=bank,
                       first=first, last=last: e.matmul(
                           avps[:, bank * 512 + j * 128: bank * 512 + j * 128 + 65],
                           exsb[:, (g % 2) * 2560 + kidx * 512 + j * 128:
                                (g % 2) * 2560 + kidx * 512 + (j + 1) * 128],
                           ap3(vpl, kc * 260 + g * 65, [[1, 65]]),
                           start=first, stop=last),
                       reads=[b_exsg[r][g % 2], b_vpslots[kc]], writes=[b_avbank[bank]],
                       inc=1 if last else 0)
            if g % 2 == 1:
                hp = g // 2
                rd, b_rd = rdn[r], b_rdn[r]
                v_(lambda e, hp=hp, rd=rd: e.reciprocal(
                    out=ap3(rd, hp * 8, [[1, 8]]),
                    in_=dataclasses.replace(
                        avps[:], offset=avps[:].offset + 64,
                        ap=[avps[:].ap[0], [128, 8], [1, 1]])),
                   reads=[b_avbank[0], b_avbank[1]], writes=[b_rd])
                v_(lambda e, hp=hp, rd=rd, r=r: e.tensor_tensor(
                    out=ap3(av16[q3], hp * 512, [[64, 8], [1, 64]]),
                    in0=dataclasses.replace(
                        avps[:], offset=avps[:].offset,
                        ap=[avps[:].ap[0], [128, 8], [1, 64]]),
                    in1=ap3(rd, hp * 8, [[1, 8], [0, 64]]), op=ALU.mult),
                   reads=[b_avbank[0], b_avbank[1], b_rd], writes=[b_av16[q3]])

        emit_scores(0)
        emit_scores(1)
        emit_av(0)
        emit_scores(2)
        emit_av(1)
        emit_scores(3)
        emit_av(2)
        emit_av(3)

    def stage_att_tail(tau):
        r = tau % 2
        q3 = tau % 3
        xi = tau % NXB
        transpose8(av16[q3], b_av16[q3], avfm[0], b_avfm[0])
        for ns in range(2):
            for ccc in range(NCH):
                t_(lambda e, ns=ns, ccc=ccc, r=r: e.matmul(
                    prj[:, ns * 512:(ns + 1) * 512],
                    avfm[0][:, ccc * 128:(ccc + 1) * 128],
                    wp[:, ccc * C + ns * 512: ccc * C + (ns + 1) * 512],
                    start=(ccc == 0), stop=(ccc == NCH - 1)),
                   reads=[b_avfm[0], b_wp], writes=[b_prjb[ns]],
                   inc=1 if ccc == NCH - 1 else 0)
            v_(lambda e, ns=ns, xi=xi: e.tensor_tensor(
                out=xb[xi][:, ns * 512:(ns + 1) * 512],
                in0=prj[:, ns * 512:(ns + 1) * 512],
                in1=xb[xi][:, ns * 512:(ns + 1) * 512], op=ALU.add),
               reads=[b_prjb[ns], b_xb[xi]], writes=[b_xb[xi]])

    def ln1_full(t2):
        r2 = t2 % 2
        ln_stats(xb[t2 % NXB], b_xb[t2 % NXB], sdum, b_sdum, r2)
        ln_apply(xb[t2 % NXB], b_xb[t2 % NXB], lntm[t2 % 4], b_lntm[t2 % 4], r2)

    N = n_ticks

    def guard(t):
        return 0 <= t < N

    x_load(0)
    ln1_full(0)
    if n_ticks > 1:
        x_load(1)
        ln1_full(1)
    stage_ln1t_pe(0)
    for it in range(N + 8):
        if guard(it):
            stage_ln1t_evac(it)
        if guard(it - 1):
            stage_qkv(it - 1)
        if guard(it - 7):
            stage_ffn_out(it - 7)
        if guard(it + 2):
            x_load(it + 2)
        if guard(it - 4):
            stage_att(it - 4)
        if guard(it - 3):
            stage_pre_b(it - 3)
        if guard(it - 5):
            stage_att_tail(it - 5)
        if guard(it - 6):
            stage_ln2(it - 6)
        if guard(it + 2):
            ln1_full(it + 2)
        if guard(it - 2):
            stage_norm_rope(it - 2)
        if guard(it + 1):
            stage_ln1t_pe(it + 1)

    for i in range(12):
        if cx.count[f"qo{i}"]:
            cx.eng["s"].wait_ge(cx.sems[f"qo{i}"], cx.count[f"qo{i}"])
    es.close()
    return nc


_ROPE_CACHE = {}


def _rope_tables(TT):
    if TT in _ROPE_CACHE:
        return _ROPE_CACHE[TT]
    import jax
    import jax.numpy as jnp
    cpu = jax.devices("cpu")[0]
    with jax.default_device(cpu):
        theta = 1.0 / 10000.0 ** (jnp.arange(0, HD, 2, dtype=jnp.float32) / HD)
        pos = jnp.arange(TT, dtype=jnp.float32)
        freqs = jnp.outer(pos, theta)
        cos = jnp.concatenate([jnp.cos(freqs), jnp.cos(freqs)], axis=-1)
        sin = jnp.concatenate([jnp.sin(freqs), jnp.sin(freqs)], axis=-1)
        cos = np.asarray(jax.device_get(cos)).astype(np.float32)
        sin = np.asarray(jax.device_get(sin)).astype(np.float32)
    sinp = sin.copy()
    sinp[:, :32] = -sinp[:, :32]
    _ROPE_CACHE[TT] = (cos, sinp)
    return cos, sinp


def make_in_maps(L0_attn_x, L0_attn_Wq, L0_attn_Wkv, L0_attn_Wproj,
                 L0_ffn_w_freq_real, L0_ffn_w_freq_imag, L0_ffn_bias,
                 n_cores=N_CORES, n_seq=N_SEQ, n_tb=N_TB):
    TT = n_tb * 128
    cos, sinp = _rope_tables(TT)
    f = np.float32
    wqt = np.ascontiguousarray(np.asarray(L0_attn_Wq, dtype=f).T)
    wkvt = np.ascontiguousarray(np.asarray(L0_attn_Wkv, dtype=f).T)
    wpt = np.ascontiguousarray(np.asarray(L0_attn_Wproj, dtype=f).T)
    wr = np.ascontiguousarray(np.asarray(L0_ffn_w_freq_real, dtype=f))
    wi = np.ascontiguousarray(np.asarray(L0_ffn_w_freq_imag, dtype=f))
    bias = np.ascontiguousarray(np.asarray(L0_ffn_bias, dtype=f).reshape(1, C))
    x = np.asarray(L0_attn_x, dtype=f)
    in_maps = []
    for c in range(n_cores):
        xs = np.ascontiguousarray(x[c * n_seq:(c + 1) * n_seq, :TT, :])
        in_maps.append({
            "x": xs, "wqt": wqt, "wkvt": wkvt, "wpt": wpt,
            "wr": wr, "wi": wi, "bias": bias, "cos": cos, "sinp": sinp,
        })
    return in_maps


def kernel(L0_attn_x, L0_attn_Wq, L0_attn_Wkv, L0_attn_Wproj,
           L0_ffn_w_freq_real, L0_ffn_w_freq_imag, L0_ffn_bias):
    in_maps = make_in_maps(L0_attn_x, L0_attn_Wq, L0_attn_Wkv, L0_attn_Wproj,
                           L0_ffn_w_freq_real, L0_ffn_w_freq_imag, L0_ffn_bias)
    nc = build_program()
    res = run_bass_kernel_spmd(nc, in_maps, core_ids=list(range(N_CORES)))
    out = np.concatenate([r["out"].reshape(N_SEQ, T, C) for r in res.results],
                         axis=0)
    return np.ascontiguousarray(out.astype(np.float32))
